# version 08
# baseline (speedup 1.0000x reference)
import contextlib
import numpy as np
import ml_dtypes
import concourse.bass as bass
import concourse.mybir as mybir
from concourse.bass_utils import run_bass_kernel_spmd

F32 = mybir.dt.float32
BF16 = mybir.dt.bfloat16
AF = mybir.ActivationFunctionType
ALU = mybir.AluOpType
AX = mybir.AxisListType

B, N, D, H, DH = 2, 8192, 1024, 8, 64
T = B * N
NCORES = 8
TPC = T // NCORES
NTC = TPC // 128
NDC = D // 128
INNER = H * DH
NE = 8
INTER = 2048
NIC = INTER // 128
EPS = 1e-5
KV_SCALE = 1.0 / (np.sqrt(float(DH)) * float(N))

BF = np.dtype(ml_dtypes.bfloat16)

SEM_NAMES = ["sw", "sxa", "sxb", "sfa", "sfb",
             "s13a", "s13b", "s13c", "s13d", "s2l",
             "smm", "svec", "sscl", "scc", "sy"]


class _InstP:
    __slots__ = ("p", "inst")

    def __init__(self, p, inst):
        self.p = p
        self.inst = inst

    def inc(self, sem, n=1):
        self.p.c[sem] = self.p.c.get(sem, 0) + n
        if self.inst is not None:
            self.inst.then_inc(self.p.sems[sem], n)
        return self


class _EngP:
    __slots__ = ("p", "name")

    def __init__(self, p, name):
        self.p = p
        self.name = name

    def wait(self, sem, val):
        if val <= 0:
            return
        if self.p.active == self.name:
            self.p.h.wait_ge(self.p.sems[sem], val)

    def dinc(self, sem, n=1):
        self.p.c[sem] = self.p.c.get(sem, 0) + n
        if self.p.active == self.name:
            self.p.h.drain().then_inc(self.p.sems[sem], n)

    def __getattr__(self, meth):
        p, name = self.p, self.name

        def call(*a, **kw):
            inst = None
            if p.active == name:
                inst = getattr(p.h, meth)(*a, **kw)
            return _InstP(p, inst)

        return call


class Prog:
    def __init__(self, sems, active, handle):
        self.sems = sems
        self.active = active
        self.h = handle
        self.c = {k: 0 for k in SEM_NAMES}
        self.t = _EngP(self, "t")
        self.v = _EngP(self, "v")
        self.s = _EngP(self, "s")
        self.g = _EngP(self, "g")
        self.sp = _EngP(self, "sp")


def build_nc(debug=False):
    nc = bass.Bass("TRN2", target_bir_lowering=False, num_devices=NCORES,
                   detect_race_conditions=False)

    def dI(name, shape, dt=F32):
        return nc.dram_tensor(name, shape, dt, kind="ExternalInput").ap()

    x_fm = dI("x_fm", [NDC, 128, TPC])
    x_tm = dI("x_tm", [NTC, 128, D])
    wqkv = dI("wqkv", [128, NDC, 3 * INNER])
    wout = dI("wout", [128, 4, D])
    gatew = dI("gatew", [128, NDC, NE])
    w13 = dI("w13", [NE, 2 * NIC, 128, NDC, 128])
    w2 = dI("w2", [NE, 128, NIC, D])
    gqb = dI("gqb", [128, INNER])
    bqb = dI("bqb", [128, INNER])
    gkb = dI("gkb", [128, INNER])
    bkb = dI("bkb", [128, INNER])
    boutb = dI("boutb", [128, D])
    ident = dI("ident", [128, 128], BF16)
    ones_col = dI("ones_col", [128, 1], BF16)
    ones_row = dI("ones_row", [1, 128], BF16)

    y_out = nc.dram_tensor("y", [NTC, 128, D], F32, kind="ExternalOutput").ap()
    cc_in = nc.dram_tensor("cc_in", [64, H * DH], F32, kind="Internal")
    cc_out = nc.dram_tensor("cc_out", [64, H * DH], F32, kind="Internal")

    dbg = {}
    if debug:
        dbg["xn"] = nc.dram_tensor("dbg_xn", [NDC, 128, TPC], BF16, kind="ExternalOutput").ap()
        dbg["qln"] = nc.dram_tensor("dbg_qln", [4, 128, TPC], BF16, kind="ExternalOutput").ap()
        dbg["comb"] = nc.dram_tensor("dbg_comb", [NTC, 128, NE], F32, kind="ExternalOutput").ap()
        dbg["kv"] = nc.dram_tensor("dbg_kv", [64, H * DH], F32, kind="ExternalOutput").ap()
        dbg["acc_attn"] = nc.dram_tensor("dbg_acc_attn", [NTC, 128, D], F32, kind="ExternalOutput").ap()

    base = ((nc.sbuf_base + 31) // 32) * 32
    top = nc.sbuf_top

    def at(name, shape, dt, off):
        return nc.alloc_sbuf_tensor_at(name, shape, dt, offset=off)

    cur = base

    def persist(name, shape, dt, nbytes):
        nonlocal cur
        t = at(name, shape, dt, cur)
        cur += nbytes
        return t

    xn = persist("xn", [128, NDC, 4, 512], BF16, NDC * 2048 * 2)
    acc = persist("acc", [128, NTC, D], F32, NTC * D * 4)
    comb = persist("comb", [128, NTC, NE], F32, NTC * NE * 4)
    OV = cur
    avail = top - 64 - OV

    o = [OV]

    def ov(name, shape, dt, nbytes):
        t = at(name, shape, dt, o[0])
        o[0] += (nbytes + 31) // 32 * 32
        return t

    xroll = ov("xroll", [128, 2, 4, 512], BF16, 2 * 2048 * 2)
    sqb = ov("sqb", [128, 2, 4, 512], BF16, 2 * 2048 * 2)
    sumrow = ov("sumrow", [1, 2048], F32, 2048 * 4)
    sqrow = ov("sqrow", [1, 2048], F32, 2048 * 4)
    mrow = ov("mrow", [1, 2048], F32, 2048 * 4)
    varrow = ov("varrow", [1, 2048], F32, 2048 * 4)
    srow = ov("srow", [1, 2048], F32, 2048 * 4)
    rrow = ov("rrow", [1, 2048], F32, 2048 * 4)
    mrow_bf = ov("mrow_bf", [1, 2048], BF16, 2048 * 2)
    rrow_bf = ov("rrow_bf", [1, 2048], BF16, 2048 * 2)
    onec_sb = ov("onec_sb", [128, 1], BF16, 32)
    oner_sb = ov("oner_sb", [1, 128], BF16, 256)
    tmpn = ov("tmpn", [128, 4, 512], F32, 2048 * 4)
    assert o[0] - OV <= avail, (o[0] - OV, avail)

    o = [OV]
    wqkv_sb = ov("wqkv_sb", [128, NDC, 3 * INNER], BF16, NDC * 3 * INNER * 2)
    wout_sb = ov("wout_sb", [128, 4, D], BF16, 4 * D * 2)
    gatew_sb = ov("gatew_sb", [128, NDC, NE], BF16, 256)
    qlnT = ov("qlnT", [128, 4, TPC], BF16, 4 * TPC * 2)
    attn_inner = ov("attn_inner", [128, 4, TPC], BF16, 4 * TPC * 2)
    qln_tm = ov("qln_tm", [128, 4, 128], BF16, 1024)
    qtmp = ov("qtmp", [128, 512], F32, 2048)
    klnb = ov("klnb", [128, 512], BF16, 1024)
    vb = ov("vb", [128, 512], BF16, 1024)
    kvloc = ov("kvloc", [64, 512], F32, 2048)
    kvz = ov("kvz", [128, 4, 128], F32, 2048)
    kvbd = ov("kvbd", [128, 4, 128], BF16, 1024)
    scores = ov("scores", [128, NTC, NE], F32, NTC * NE * 4)
    scores2 = ov("scores2", [128, NTC, NE], F32, NTC * NE * 4)
    prob8 = ov("prob8", [128, NE], F32, 32)
    sm1 = ov("sm1", [128, 16], F32, 64)
    sm2 = ov("sm2", [128, 16], F32, 64)
    sm3 = ov("sm3", [128, 16], F32, 64)
    sm4 = ov("sm4", [128, 16], F32, 64)
    sm5 = ov("sm5", [128, 16], F32, 64)
    sm6 = ov("sm6", [128, 16], F32, 64)
    gs4 = ov("gs4", [128, 4], F32, 16)
    gs4b = ov("gs4b", [128, 4], F32, 16)
    gsh = ov("gsh", [128, 2], F32, 8)
    gsl = ov("gsl", [128, 2], F32, 8)
    msk8 = ov("msk8", [128, NE], F32, 32)
    msk8b = ov("msk8b", [128, NE], F32, 32)
    ident_sb = ov("ident_sb", [128, 128], BF16, 256)
    gqb_sb = ov("gqb_sb", [128, INNER], F32, INNER * 4)
    bqb_sb = ov("bqb_sb", [128, INNER], F32, INNER * 4)
    gkb_sb = ov("gkb_sb", [128, INNER], F32, INNER * 4)
    bkb_sb = ov("bkb_sb", [128, INNER], F32, INNER * 4)
    boutb_sb = ov("boutb_sb", [128, D], F32, D * 4)
    assert o[0] - OV <= avail, (o[0] - OV, avail)

    o = [OV]
    act = ov("act", [128, NIC, TPC], BF16, NIC * TPC * 2)
    w2_sb = ov("w2_sb", [128, NIC, D], BF16, NIC * D * 2)
    w13_off = o[0]
    w13_sb = ov("w13_sb", [128, 4, NDC, 128], BF16, 4 * NDC * 128 * 2)
    silu_sb = ov("silu_sb", [128, 512], F32, 512 * 4)
    assert o[0] - OV <= avail, (o[0] - OV, avail)
    xfin = at("xfin", [128, 2, D], F32, w13_off)

    PS = nc.alloc_psum_tensor("PS", [128, 8, 512], F32)

    def emit(P: Prog):
        t, v, s, g, sp = P.t, P.v, P.s, P.g, P.sp
        c = P.c

        g.dma_start(out=onec_sb[:], in_=ones_col).inc("sw", 16)
        g.dma_start(out=oner_sb[:], in_=ones_row).inc("sw", 16)
        consts_loaded = c["sw"]

        xslot = ["sxa", "sxb"]
        sq_done, stat_mm = [], []
        t.wait("sw", consts_loaded)
        for cn in range(NDC):
            sl = cn % 2
            if cn >= 2:
                g.wait("smm", stat_mm[cn - 2])
                g.wait("sscl", sq_done[cn - 2])
            g.dma_start(out=xroll[:, sl],
                        in_=x_fm[cn].rearrange("p (a b) -> p a b", a=4)).inc(xslot[sl], 16)

            s.wait(xslot[sl], 16 * (cn // 2 + 1))
            if cn >= 2:
                s.wait("smm", stat_mm[cn - 2])
            s.activation(out=sqb[:, sl].rearrange("p a b -> p (a b)"),
                         in_=xroll[:, sl].rearrange("p a b -> p (a b)"),
                         func=AF.Square)
            s.dinc("sscl", 1)
            sq_done.append(c["sscl"])

            t.wait(xslot[sl], 16 * (cn // 2 + 1))
            for tt in range(4):
                t.matmul(PS[0:1, tt, :], onec_sb[:], xroll[:, sl, tt, :],
                         start=(cn == 0), stop=(cn == NDC - 1))
            t.wait("sscl", sq_done[cn])
            for tt in range(4):
                mm = t.matmul(PS[0:1, 4 + tt, :], onec_sb[:], sqb[:, sl, tt, :],
                              start=(cn == 0), stop=(cn == NDC - 1))
            mm.inc("smm", 1)
            stat_mm.append(c["smm"])

        v.wait("smm", c["smm"])
        v.tensor_copy(sumrow[:], PS[0:1, 0:4, :].rearrange("p a b -> p (a b)"))
        v.tensor_copy(sqrow[:], PS[0:1, 4:8, :].rearrange("p a b -> p (a b)"))
        v.drain()
        v.tensor_scalar_mul(mrow[:], sumrow[:], 1.0 / D)
        v.tensor_scalar_mul(sqrow[:], sqrow[:], 1.0 / D)
        v.drain()
        v.tensor_mul(sumrow[:], mrow[:], mrow[:])
        v.drain()
        v.tensor_sub(varrow[:], sqrow[:], sumrow[:])
        v.drain()
        v.tensor_scalar_add(varrow[:], varrow[:], EPS)
        v.dinc("svec", 1)
        s.wait("svec", c["svec"])
        s.activation(out=srow[:], in_=varrow[:], func=AF.Sqrt)
        s.dinc("sscl", 1)
        v.wait("sscl", c["sscl"])
        v.reciprocal(rrow[:], srow[:])
        v.drain()
        v.tensor_copy(mrow_bf[:], mrow[:])
        v.tensor_copy(rrow_bf[:], rrow[:])
        v.dinc("svec", 1)
        rows_done = c["svec"]

        t.wait("svec", rows_done)
        for tt in range(4):
            t.matmul(PS[:, tt, :], oner_sb[:], mrow_bf[0:1, tt * 512:(tt + 1) * 512],
                     start=True, stop=True)
            mm = t.matmul(PS[:, 4 + tt, :], oner_sb[:], rrow_bf[0:1, tt * 512:(tt + 1) * 512],
                          start=True, stop=True)
        mm.inc("smm", 1)
        bcast_mm = c["smm"]

        norm_done = []
        for cn in range(NDC):
            sl = cn % 2
            g.wait("smm", stat_mm[NDC - 1])
            g.wait("sscl", sq_done[NDC - 1])
            if cn >= 2:
                g.wait("svec", norm_done[cn - 2])
            g.dma_start(out=xroll[:, sl],
                        in_=x_fm[cn].rearrange("p (a b) -> p a b", a=4)).inc(xslot[sl], 16)

            v.wait(xslot[sl], 16 * (NDC // 2 + cn // 2 + 1))
            v.wait("smm", bcast_mm)
            v.tensor_sub(tmpn[:], xroll[:, sl], PS[:, 0:4, :])
            v.drain()
            v.tensor_mul(xn[:, cn], tmpn[:], PS[:, 4:8, :])
            v.dinc("svec", 1)
            norm_done.append(c["svec"])
        xn_done = c["svec"]

        g.wait("svec", xn_done)
        g.dma_start(out=ident_sb[:], in_=ident).inc("sw", 16)
        g.dma_start(out=wqkv_sb[:], in_=wqkv).inc("sw", 16)
        g.dma_start(out=wout_sb[:], in_=wout).inc("sw", 16)
        g.dma_start(out=gatew_sb[:], in_=gatew).inc("sw", 16)
        g.dma_start(out=gqb_sb[:], in_=gqb).inc("sw", 16)
        g.dma_start(out=bqb_sb[:], in_=bqb).inc("sw", 16)
        g.dma_start(out=gkb_sb[:], in_=gkb).inc("sw", 16)
        g.dma_start(out=bkb_sb[:], in_=bkb).inc("sw", 16)
        g.dma_start(out=boutb_sb[:], in_=boutb).inc("sw", 16)
        p0b_w = c["sw"]

        if debug:
            for cn in range(NDC):
                sp.wait("svec", xn_done)
                sp.dma_start(out=dbg["xn"][cn],
                             in_=xn[:, cn].rearrange("p a b -> p (a b)")).inc("sy", 16)

        def head_ln(bank, mm_cnt, gb, bb, out_bf):
            v.wait("smm", mm_cnt)
            v.reduce_sum(sm1[:, 0:8].rearrange("p (g w) -> p g w", w=1),
                         PS[:, bank, :].rearrange("p (g w) -> p g w", g=8), axis=AX.X)
            v.drain()
            v.tensor_scalar_mul(sm1[:, 0:8], sm1[:, 0:8], 1.0 / DH)
            v.dinc("svec", 1)
            s.wait("smm", mm_cnt)
            s.wait("svec", c["svec"])
            for h in range(8):
                s.activation(out=qtmp[:, h * 64:(h + 1) * 64],
                             in_=PS[:, bank, h * 64:(h + 1) * 64],
                             func=AF.Square, accum_out=sm2[:, h:h + 1])
            s.dinc("sscl", 1)
            v.wait("sscl", c["sscl"])
            v.tensor_scalar_mul(sm3[:, 0:8], sm2[:, 0:8], 1.0 / DH)
            v.tensor_mul(sm4[:, 0:8], sm1[:, 0:8], sm1[:, 0:8])
            v.drain()
            v.tensor_sub(sm3[:, 0:8], sm3[:, 0:8], sm4[:, 0:8])
            v.drain()
            v.tensor_scalar_add(sm3[:, 0:8], sm3[:, 0:8], EPS)
            v.dinc("svec", 1)
            s.wait("svec", c["svec"])
            s.activation(out=sm4[:, 0:8], in_=sm3[:, 0:8], func=AF.Sqrt)
            s.dinc("sscl", 1)
            v.wait("sscl", c["sscl"])
            v.reciprocal(sm5[:, 0:8], sm4[:, 0:8])
            v.drain()
            for h in range(8):
                v.tensor_scalar(qtmp[:, h * 64:(h + 1) * 64], PS[:, bank, h * 64:(h + 1) * 64],
                                sm1[:, h:h + 1], sm5[:, h:h + 1], ALU.subtract, ALU.mult)
            v.drain()
            v.tensor_mul(qtmp[:], qtmp[:], gb[:])
            v.drain()
            v.tensor_add(out_bf, qtmp[:], bb[:])
            v.dinc("svec", 1)

        t.wait("sw", p0b_w)
        qkv_evac = []
        kv_last_mm = 0
        for tc in range(NTC):
            ts128 = slice(tc * 128, (tc + 1) * 128)
            xn_tile = lambda dc: xn[:, dc, tc // 4, (tc % 4) * 128:(tc % 4) * 128 + 128]
            if tc == 0:
                t.wait("svec", xn_done)
            else:
                t.wait("svec", qkv_evac[tc - 1])
            for bank, fo0 in ((0, INNER), (1, 2 * INNER), (2, 0)):
                for dc in range(NDC):
                    mm = t.matmul(PS[:, bank, :], xn_tile(dc), wqkv_sb[:, dc, fo0:fo0 + 512],
                                  start=(dc == 0), stop=(dc == NDC - 1))
                mm.inc("smm", 1)
            for dc in range(NDC):
                mm = t.matmul(PS[:, 3, 0:NE], xn_tile(dc), gatew_sb[:, dc, :],
                              start=(dc == 0), stop=(dc == NDC - 1))
            mm.inc("smm", 1)
            qkv_mm = c["smm"]

            head_ln(0, qkv_mm - 3, gkb_sb, bkb_sb, klnb[:])
            v.wait("smm", qkv_mm - 2)
            v.tensor_copy(vb[:], PS[:, 1, :])
            v.dinc("svec", 1)
            t.wait("svec", c["svec"])
            for h in range(8):
                mm = t.matmul(PS[0:64, 4, h * 64:(h + 1) * 64], klnb[:, h * 64:(h + 1) * 64],
                              vb[:, h * 64:(h + 1) * 64],
                              start=(tc == 0 and h == 0),
                              stop=(tc == NTC - 1 and h == 7))
            mm.inc("smm", 1)
            kv_last_mm = c["smm"]

            head_ln(2, qkv_mm - 1, gqb_sb, bqb_sb,
                    qln_tm[:].rearrange("p a b -> p (a b)"))
            for fc in range(4):
                t.wait("svec", c["svec"])
                t.transpose(PS[:, 5, 0:64].bitcast(BF16), qln_tm[:, fc, :],
                            ident_sb[:]).inc("smm", 1)
                v.wait("smm", c["smm"])
                v.tensor_copy(qlnT[:, fc, ts128], PS[:, 5, 0:64].bitcast(BF16))
                v.dinc("svec", 1)

            v.wait("smm", qkv_mm)
            v.tensor_copy(scores[:, tc, :], PS[:, 3, 0:NE])
            v.dinc("svec", 1)
            qkv_evac.append(c["svec"])

        for tc in range(NTC):
            v.reduce_max(sm1[:, tc:tc + 1], scores[:, tc, :], axis=AX.X)
        v.drain()
        v.tensor_scalar_mul(sm2[:], sm1[:], -1.0)
        v.dinc("svec", 1)
        s.wait("svec", c["svec"])
        for tc in range(NTC):
            s.activation(out=scores2[:, tc, :], in_=scores[:, tc, :], func=AF.Exp,
                         bias=sm2[:, tc:tc + 1], accum_out=sm3[:, tc:tc + 1])
        s.dinc("sscl", 1)
        v.wait("sscl", c["sscl"])
        v.reciprocal(sm5[:], sm3[:])
        v.drain()
        for tc in range(NTC):
            v.tensor_scalar_mul(prob8[:], scores2[:, tc, :], sm5[:, tc:tc + 1])
            v.drain()
            v.tensor_tensor(gs4[:], prob8[:, 0:8:2], prob8[:, 1:8:2], ALU.max)
            v.drain()
            v.tensor_tensor(gsh[:], gs4[:, 0:4:2], gs4[:, 1:4:2], ALU.max)
            v.tensor_tensor(gsl[:], gs4[:, 0:4:2], gs4[:, 1:4:2], ALU.min)
            v.drain()
            v.tensor_tensor(sm6[:, 0:1], gsh[:, 0:1], gsh[:, 1:2], ALU.min)
            v.tensor_tensor(sm6[:, 1:2], gsl[:, 0:1], gsl[:, 1:2], ALU.max)
            v.drain()
            v.tensor_tensor(sm6[:, 2:3], sm6[:, 0:1], sm6[:, 1:2], ALU.max)
            v.drain()
            v.tensor_single_scalar(gs4[:], gs4[:], sm6[:, 2:3], ALU.is_ge)
            v.drain()
            v.tensor_mul(msk8[:, 0:8:2], prob8[:, 0:8:2], gs4[:])
            v.tensor_mul(msk8[:, 1:8:2], prob8[:, 1:8:2], gs4[:])
            v.drain()
            v.reduce_max(sm6[:, 3:4], msk8[:], axis=AX.X)
            v.drain()
            v.tensor_single_scalar(msk8b[:], msk8[:], sm6[:, 3:4], ALU.is_equal)
            v.drain()
            v.tensor_scalar(msk8b[:], msk8b[:], -1.0, 1.0, ALU.mult, ALU.add)
            v.drain()
            v.tensor_mul(msk8b[:], msk8[:], msk8b[:])
            v.drain()
            v.reduce_max(sm6[:, 4:5], msk8b[:], axis=AX.X)
            v.drain()
            v.tensor_add(sm6[:, 5:6], sm6[:, 3:4], sm6[:, 4:5])
            v.drain()
            v.reciprocal(sm6[:, 6:7], sm6[:, 5:6])
            v.tensor_single_scalar(msk8b[:], msk8[:], sm6[:, 4:5], ALU.is_ge)
            v.drain()
            v.scalar_tensor_tensor(msk8[:], msk8[:], sm6[:, 6:7], msk8b[:],
                                   ALU.mult, ALU.mult)
            v.drain()
            v.tensor_copy(comb[:, tc, :], msk8[:])
        v.dinc("svec", 1)
        comb_done = c["svec"]

        if debug:
            sp.wait("svec", comb_done)
            for tc in range(NTC):
                sp.dma_start(out=dbg["comb"][tc], in_=comb[:, tc, :]).inc("sy", 16)
            for fc in range(4):
                sp.dma_start(out=dbg["qln"][fc], in_=qlnT[:, fc, :]).inc("sy", 16)

        slot_sems = ["s13a", "s13b", "s13c", "s13d"]
        slot_uses = [0, 0, 0, 0]
        slot_freed_at = [0, 0, 0, 0]

        def issue_w13(e, j):
            for half, sidx in ((0, (2 * j) % 4), (1, (2 * j + 1) % 4)):
                g.wait("smm", slot_freed_at[sidx])
                g.dma_start(out=w13_sb[:, sidx],
                            in_=w13[e, half * NIC + j]).inc(slot_sems[sidx], 16)
                slot_uses[sidx] += 1

        issue_w13(0, 0)
        issue_w13(0, 1)

        v.wait("smm", kv_last_mm)
        v.tensor_copy(kvloc[:], PS[0:64, 4, :])
        v.memset(kvz[:], 0.0)
        v.dinc("svec", 1)
        kv_evac_v = c["svec"]
        cc_state = {}

        def emit_cc():
            g.wait("svec", kv_evac_v)
            g.dma_start(out=cc_in.ap(), in_=kvloc[:]).inc("sw", 16)
            g.wait("sw", c["sw"])
            g.collective_compute(
                "AllReduce", ALU.add,
                replica_groups=[[0, 1, 2, 3], [4, 5, 6, 7]],
                ins=[cc_in.ap()],
                outs=[cc_out.ap()],
            ).inc("scc", 1)
            g.wait("scc", 1)
            for cp in range(4):
                g.dma_start(out=kvz[0:64, cp, 0:64],
                            in_=cc_out.ap()[0:64, (2 * cp) * 64:(2 * cp + 1) * 64]).inc("sw", 16)
                g.dma_start(out=kvz[64:128, cp, 64:128],
                            in_=cc_out.ap()[0:64, (2 * cp + 1) * 64:(2 * cp + 2) * 64]).inc("sw", 16)
            cc_state["ld"] = c["sw"]


        attn_state = {}

        def emit_attention():
            v.wait("sw", cc_state["ld"])
            v.tensor_scalar_mul(kvbd[:].rearrange("p a b -> p (a b)"),
                                kvz[:].rearrange("p a b -> p (a b)"), KV_SCALE)
            v.dinc("svec", 1)
            kvbf_done = c["svec"]
            t.wait("svec", kvbf_done)
            att_base = c["svec"]
            for idx in range(16):
                cp, tt = idx // 4, idx % 4
                bank = 6 + idx % 2
                tsl = slice(tt * 512, (tt + 1) * 512)
                if idx >= 2:
                    t.wait("svec", att_base + idx - 1)
                t.matmul(PS[:, bank, :], kvbd[:, cp, :], qlnT[:, cp, tsl],
                         start=True, stop=True).inc("smm", 1)
                v.wait("smm", c["smm"])
                v.tensor_copy(attn_inner[:, cp, tsl], PS[:, bank, :])
                v.dinc("svec", 1)
            att_evac = c["svec"]

            for tc in range(NTC):
                ts128 = slice(tc * 128, (tc + 1) * 128)
                for ht in range(2):
                    idx = 2 * tc + ht
                    bank = 6 + idx % 2
                    if idx >= 2:
                        t.wait("svec", att_evac + idx - 1)
                    else:
                        t.wait("svec", att_evac)
                    for ic in range(4):
                        mm = t.matmul(PS[:, bank, :], attn_inner[:, ic, ts128],
                                      wout_sb[:, ic, ht * 512:(ht + 1) * 512],
                                      start=(ic == 0), stop=(ic == 3))
                    mm.inc("smm", 1)
                    v.wait("smm", c["smm"])
                    v.tensor_tensor(acc[:, tc, ht * 512:(ht + 1) * 512], PS[:, bank, :],
                                    boutb_sb[:, ht * 512:(ht + 1) * 512], ALU.add)
                    v.dinc("svec", 1)
            attn_state["done_v"] = c["svec"]
            attn_state["done_t"] = c["smm"]

        w2_loads = 0
        p2_last = 0

        def issue_w2_part(e, part):
            nonlocal w2_loads
            if e == 0:
                g.wait("svec", attn_state["done_v"])
            else:
                g.wait("smm", p2_last)
            g.dma_start(out=w2_sb[:, 4 * part:4 * part + 4, :],
                        in_=w2[e][:, 4 * part:4 * part + 4, :]).inc("s2l", 16)
            w2_loads += 1

        unit_done = []
        for e in range(NE):
            for j in range(NIC):
                if not (e == 0 and j < 2):
                    issue_w13(e, j)
                if e == 0:
                    if j == 3:
                        emit_cc()
                    if j in (6, 8, 10, 12):
                        issue_w2_part(e, (j - 6) // 2)
                else:
                    if j in (1, 3, 5, 7):
                        issue_w2_part(e, (j - 1) // 2)

                sg, su = (2 * j) % 4, (2 * j + 1) % 4
                t.wait(slot_sems[sg], 16 * slot_uses[sg])
                t.wait(slot_sems[su], 16 * slot_uses[su])
                if e == 0 and j == 0:
                    t.wait("svec", kv_evac_v)
                for tt in range(4):
                    bb = (len(unit_done) % 3) * 2
                    if len(unit_done) >= 3:
                        t.wait("sscl", unit_done[-3][0])
                        t.wait("svec", unit_done[-3][1])
                    for dc in range(NDC):
                        mm = t.matmul(PS[:, bb, :], w13_sb[:, sg, dc, :], xn[:, dc, tt, :],
                                      start=(dc == 0), stop=(dc == NDC - 1))
                    mm.inc("smm", 1)
                    gate_cnt = c["smm"]
                    for dc in range(NDC):
                        mm = t.matmul(PS[:, bb + 1, :], w13_sb[:, su, dc, :], xn[:, dc, tt, :],
                                      start=(dc == 0), stop=(dc == NDC - 1))
                    mm.inc("smm", 1)
                    up_cnt = c["smm"]
                    s.wait("smm", gate_cnt)
                    s.wait("svec", c["svec"])
                    s.activation(out=silu_sb[:], in_=PS[:, bb, :], func=AF.Silu)
                    s.dinc("sscl", 1)
                    v.wait("sscl", c["sscl"])
                    v.wait("smm", up_cnt)
                    v.tensor_mul(act[:, j, tt * 512:(tt + 1) * 512], silu_sb[:],
                                 PS[:, bb + 1, :])
                    v.dinc("svec", 1)
                    unit_done.append((c["sscl"], c["svec"]))
                slot_freed_at[sg] = up_cnt
                slot_freed_at[su] = up_cnt
                if e == 0 and j == 5:
                    emit_attention()
            act_done = c["svec"]

            t.wait("s2l", 16 * w2_loads)
            t.wait("svec", act_done)
            p2_base = c["svec"]
            for tc in range(NTC):
                ts128 = slice(tc * 128, (tc + 1) * 128)
                for ht in range(2):
                    idx = 2 * tc + ht
                    bank = 6 + idx % 2
                    if idx >= 2:
                        t.wait("svec", p2_base + idx - 1)
                    for ic in range(NIC):
                        mm = t.matmul(PS[:, bank, :], act[:, ic, ts128],
                                      w2_sb[:, ic, ht * 512:(ht + 1) * 512],
                                      start=(ic == 0), stop=(ic == NIC - 1))
                    mm.inc("smm", 1)
                    v.wait("smm", c["smm"])
                    v.scalar_tensor_tensor(acc[:, tc, ht * 512:(ht + 1) * 512],
                                           PS[:, bank, :], comb[:, tc, e:e + 1],
                                           acc[:, tc, ht * 512:(ht + 1) * 512],
                                           ALU.mult, ALU.add)
                    v.dinc("svec", 1)
            p2_last = c["smm"]

        ffn_done_v = c["svec"]

        fslot = ["sfa", "sfb"]
        fuses = [0, 0]
        for tc in range(NTC):
            sl = tc % 2
            if tc < 2:
                sp.wait("smm", p2_last)
            else:
                sp.wait("svec", ffn_done_v + tc - 1)
            sp.dma_start(out=xfin[:, sl], in_=x_tm[tc]).inc(fslot[sl], 16)
            fuses[sl] += 1
            v.wait(fslot[sl], 16 * fuses[sl])
            v.tensor_add(acc[:, tc, :], acc[:, tc, :], xfin[:, sl, :])
            v.dinc("svec", 1)
            sp.wait("svec", c["svec"])
            sp.dma_start(out=y_out[tc], in_=acc[:, tc, :]).inc("sy", 16)
        sp.wait("sy", c["sy"])

    with contextlib.ExitStack() as es:
        sems = {nm: es.enter_context(nc.semaphore(nm)) for nm in SEM_NAMES}
        block = es.enter_context(nc.Block())

        @block.tensor
        def _(h):
            emit(Prog(sems, "t", h))

        @block.vector
        def _(h):
            emit(Prog(sems, "v", h))

        @block.scalar
        def _(h):
            emit(Prog(sems, "s", h))

        @block.gpsimd
        def _(h):
            emit(Prog(sems, "g", h))

        @block.sync
        def _(h):
            emit(Prog(sems, "sp", h))

    return nc


def _prep_core_inputs(inputs):
    x = np.asarray(inputs["L0_attn_x"], np.float32).reshape(T, D)
    W_qkv = np.asarray(inputs["L0_attn_W_qkv"], np.float32)
    gq = np.asarray(inputs["L0_attn_gamma_q"], np.float32).reshape(INNER)
    bq = np.asarray(inputs["L0_attn_beta_q"], np.float32).reshape(INNER)
    gk = np.asarray(inputs["L0_attn_gamma_k"], np.float32).reshape(INNER)
    bk = np.asarray(inputs["L0_attn_beta_k"], np.float32).reshape(INNER)
    W_out = np.asarray(inputs["L0_attn_W_out"], np.float32)
    b_out = np.asarray(inputs["L0_attn_b_out"], np.float32)
    gate_w = np.asarray(inputs["L0_ffn_gate_w"], np.float32)
    w13 = np.asarray(inputs["L0_ffn_w13_weight"], np.float32)
    w2 = np.asarray(inputs["L0_ffn_w2_weight"], np.float32)

    wqkv_h = np.ascontiguousarray(W_qkv.T.reshape(NDC, 128, 3 * INNER).transpose(1, 0, 2))
    wout_h = np.ascontiguousarray(W_out.T.reshape(4, 128, D).transpose(1, 0, 2))
    gatew_h = np.ascontiguousarray(gate_w.T.reshape(NDC, 128, NE).transpose(1, 0, 2))
    w13_h = np.ascontiguousarray(
        w13.transpose(0, 2, 1).reshape(NE, NDC, 128, 2 * NIC, 128).transpose(0, 3, 2, 1, 4))
    w2_h = np.ascontiguousarray(
        w2.transpose(0, 2, 1).reshape(NE, NIC, 128, D).transpose(0, 2, 1, 3))
    gqb_h = np.ascontiguousarray(np.broadcast_to(gq, (128, INNER)))
    bqb_h = np.ascontiguousarray(np.broadcast_to(bq, (128, INNER)))
    gkb_h = np.ascontiguousarray(np.broadcast_to(gk, (128, INNER)))
    bkb_h = np.ascontiguousarray(np.broadcast_to(bk, (128, INNER)))
    boutb_h = np.ascontiguousarray(np.broadcast_to(b_out, (128, D)))
    ident_h = np.eye(128, dtype=BF)
    ones_col_h = np.ones((128, 1), BF)
    ones_row_h = np.ones((1, 128), BF)

    in_maps = []
    for cid in range(NCORES):
        sl = x[cid * TPC:(cid + 1) * TPC]
        in_maps.append({
            "x_fm": np.ascontiguousarray(sl.T).reshape(NDC, 128, TPC),
            "x_tm": np.ascontiguousarray(sl).reshape(NTC, 128, D),
            "wqkv": wqkv_h, "wout": wout_h, "gatew": gatew_h,
            "w13": w13_h, "w2": w2_h,
            "gqb": gqb_h, "bqb": bqb_h, "gkb": gkb_h, "bkb": bkb_h,
            "boutb": boutb_h, "ident": ident_h,
            "ones_col": ones_col_h, "ones_row": ones_row_h,
        })
    return in_maps


_NC_CACHE = {}


def kernel(**inputs):
    if "nc" not in _NC_CACHE:
        _NC_CACHE["nc"] = build_nc(debug=False)
    nc = _NC_CACHE["nc"]
    in_maps = _prep_core_inputs(inputs)
    res = run_bass_kernel_spmd(nc, in_maps, core_ids=list(range(NCORES)))
    out = np.empty((T, D), np.float32)
    for cid in range(NCORES):
        out[cid * TPC:(cid + 1) * TPC] = res.results[cid]["y"].reshape(TPC, D)
    return out.reshape(B, N, D)


# version 09
# speedup vs baseline: 1.0261x; 1.0261x over previous
import contextlib
import numpy as np
import ml_dtypes
import concourse.bass as bass
import concourse.mybir as mybir
from concourse.bass_utils import run_bass_kernel_spmd

F32 = mybir.dt.float32
BF16 = mybir.dt.bfloat16
AF = mybir.ActivationFunctionType
ALU = mybir.AluOpType
AX = mybir.AxisListType

B, N, D, H, DH = 2, 8192, 1024, 8, 64
T = B * N
NCORES = 8
TPC = T // NCORES
NTC = TPC // 128
NDC = D // 128
INNER = H * DH
NE = 8
INTER = 2048
NIC = INTER // 128
EPS = 1e-5
KV_SCALE = 1.0 / (np.sqrt(float(DH)) * float(N))

BF = np.dtype(ml_dtypes.bfloat16)

SEM_NAMES = ["sw", "sxa", "sxb", "sfa", "sfb",
             "s13a", "s13b", "s13c", "s13d", "s2l",
             "smm", "svec", "sscl", "scc", "sy"]


class _InstP:
    __slots__ = ("p", "inst")

    def __init__(self, p, inst):
        self.p = p
        self.inst = inst

    def inc(self, sem, n=1):
        self.p.c[sem] = self.p.c.get(sem, 0) + n
        if self.inst is not None:
            self.inst.then_inc(self.p.sems[sem], n)
        return self


class _EngP:
    __slots__ = ("p", "name")

    def __init__(self, p, name):
        self.p = p
        self.name = name

    def wait(self, sem, val):
        if val <= 0:
            return
        if self.p.active == self.name:
            self.p.h.wait_ge(self.p.sems[sem], val)

    def dinc(self, sem, n=1):
        self.p.c[sem] = self.p.c.get(sem, 0) + n
        if self.p.active == self.name:
            self.p.h.drain().then_inc(self.p.sems[sem], n)

    def __getattr__(self, meth):
        p, name = self.p, self.name

        def call(*a, **kw):
            inst = None
            if p.active == name:
                inst = getattr(p.h, meth)(*a, **kw)
            return _InstP(p, inst)

        return call


class Prog:
    def __init__(self, sems, active, handle):
        self.sems = sems
        self.active = active
        self.h = handle
        self.c = {k: 0 for k in SEM_NAMES}
        self.t = _EngP(self, "t")
        self.v = _EngP(self, "v")
        self.s = _EngP(self, "s")
        self.g = _EngP(self, "g")
        self.sp = _EngP(self, "sp")


def build_nc(debug=False):
    nc = bass.Bass("TRN2", target_bir_lowering=False, num_devices=NCORES,
                   detect_race_conditions=False)

    def dI(name, shape, dt=F32):
        return nc.dram_tensor(name, shape, dt, kind="ExternalInput").ap()

    x_fm = dI("x_fm", [NDC, 128, TPC])
    x_tm = dI("x_tm", [NTC, 128, D])
    wqkv = dI("wqkv", [128, NDC, 3 * INNER])
    wout = dI("wout", [128, 4, D])
    gatew = dI("gatew", [128, NDC, NE])
    w13 = dI("w13", [NE, 2 * NIC, 128, NDC, 128])
    w2 = dI("w2", [NE, 128, NIC, D])
    gqb = dI("gqb", [128, INNER])
    bqb = dI("bqb", [128, INNER])
    gkb = dI("gkb", [128, INNER])
    bkb = dI("bkb", [128, INNER])
    boutb = dI("boutb", [128, D])
    ident = dI("ident", [128, 128], BF16)
    ones_col = dI("ones_col", [128, 1], BF16)
    ones_row = dI("ones_row", [1, 128], BF16)

    y_out = nc.dram_tensor("y", [NTC, 128, D], F32, kind="ExternalOutput").ap()
    cc_in = nc.dram_tensor("cc_in", [64, H * DH], F32, kind="Internal")
    cc_out = nc.dram_tensor("cc_out", [64, H * DH], F32, kind="Internal")

    dbg = {}
    if debug:
        dbg["xn"] = nc.dram_tensor("dbg_xn", [NDC, 128, TPC], BF16, kind="ExternalOutput").ap()
        dbg["qln"] = nc.dram_tensor("dbg_qln", [4, 128, TPC], BF16, kind="ExternalOutput").ap()
        dbg["comb"] = nc.dram_tensor("dbg_comb", [NTC, 128, NE], F32, kind="ExternalOutput").ap()
        dbg["kv"] = nc.dram_tensor("dbg_kv", [64, H * DH], F32, kind="ExternalOutput").ap()
        dbg["acc_attn"] = nc.dram_tensor("dbg_acc_attn", [NTC, 128, D], F32, kind="ExternalOutput").ap()

    base = ((nc.sbuf_base + 31) // 32) * 32
    top = nc.sbuf_top

    def at(name, shape, dt, off):
        return nc.alloc_sbuf_tensor_at(name, shape, dt, offset=off)

    cur = base

    def persist(name, shape, dt, nbytes):
        nonlocal cur
        t = at(name, shape, dt, cur)
        cur += nbytes
        return t

    xn = persist("xn", [128, NDC, 4, 512], BF16, NDC * 2048 * 2)
    acc = persist("acc", [128, NTC, D], F32, NTC * D * 4)
    comb = persist("comb", [128, NTC, NE], F32, NTC * NE * 4)
    OV = cur
    avail = top - 64 - OV

    o = [OV]

    def ov(name, shape, dt, nbytes):
        t = at(name, shape, dt, o[0])
        o[0] += (nbytes + 31) // 32 * 32
        return t

    xroll = ov("xroll", [128, 2, 4, 512], BF16, 2 * 2048 * 2)
    sqb = ov("sqb", [128, 2, 4, 512], BF16, 2 * 2048 * 2)
    sumrow = ov("sumrow", [1, 2048], F32, 2048 * 4)
    sqrow = ov("sqrow", [1, 2048], F32, 2048 * 4)
    mrow = ov("mrow", [1, 2048], F32, 2048 * 4)
    varrow = ov("varrow", [1, 2048], F32, 2048 * 4)
    srow = ov("srow", [1, 2048], F32, 2048 * 4)
    rrow = ov("rrow", [1, 2048], F32, 2048 * 4)
    mrow_bf = ov("mrow_bf", [1, 2048], BF16, 2048 * 2)
    rrow_bf = ov("rrow_bf", [1, 2048], BF16, 2048 * 2)
    onec_sb = ov("onec_sb", [128, 1], BF16, 32)
    oner_sb = ov("oner_sb", [1, 128], BF16, 256)
    tmpn = ov("tmpn", [128, 4, 512], F32, 2048 * 4)
    assert o[0] - OV <= avail, (o[0] - OV, avail)

    o = [OV]
    wqkv_sb = ov("wqkv_sb", [128, NDC, 3 * INNER], BF16, NDC * 3 * INNER * 2)
    wout_sb = ov("wout_sb", [128, 4, D], BF16, 4 * D * 2)
    gatew_sb = ov("gatew_sb", [128, NDC, NE], BF16, 256)
    qlnT = ov("qlnT", [128, 4, TPC], BF16, 4 * TPC * 2)
    attn_inner = ov("attn_inner", [128, 4, TPC], BF16, 4 * TPC * 2)
    qln_tm = ov("qln_tm", [128, 4, 128], BF16, 1024)
    qtmp = ov("qtmp", [128, 512], F32, 2048)
    klnb = ov("klnb", [128, 512], BF16, 1024)
    vb = ov("vb", [128, 512], BF16, 1024)
    kvloc = ov("kvloc", [64, 512], F32, 2048)
    kvz = ov("kvz", [128, 4, 128], F32, 2048)
    kvbd = ov("kvbd", [128, 4, 128], BF16, 1024)
    scores = ov("scores", [128, NTC, NE], F32, NTC * NE * 4)
    scores2 = ov("scores2", [128, NTC, NE], F32, NTC * NE * 4)
    prob8 = ov("prob8", [128, NE], F32, 32)
    sm1 = ov("sm1", [128, 16], F32, 64)
    sm2 = ov("sm2", [128, 16], F32, 64)
    sm3 = ov("sm3", [128, 16], F32, 64)
    sm4 = ov("sm4", [128, 16], F32, 64)
    sm5 = ov("sm5", [128, 16], F32, 64)
    sm6 = ov("sm6", [128, 16], F32, 64)
    gs4 = ov("gs4", [128, 4], F32, 16)
    gs4b = ov("gs4b", [128, 4], F32, 16)
    gsh = ov("gsh", [128, 2], F32, 8)
    gsl = ov("gsl", [128, 2], F32, 8)
    msk8 = ov("msk8", [128, NE], F32, 32)
    msk8b = ov("msk8b", [128, NE], F32, 32)
    ident_sb = ov("ident_sb", [128, 128], BF16, 256)
    gqb_sb = ov("gqb_sb", [128, INNER], F32, INNER * 4)
    bqb_sb = ov("bqb_sb", [128, INNER], F32, INNER * 4)
    gkb_sb = ov("gkb_sb", [128, INNER], F32, INNER * 4)
    bkb_sb = ov("bkb_sb", [128, INNER], F32, INNER * 4)
    boutb_sb = ov("boutb_sb", [128, D], F32, D * 4)
    assert o[0] - OV <= avail, (o[0] - OV, avail)

    o = [OV]
    act = ov("act", [128, NIC, TPC], BF16, NIC * TPC * 2)
    w2_sb = ov("w2_sb", [128, NIC, D], BF16, NIC * D * 2)
    w13_off = o[0]
    w13_sb = ov("w13_sb", [128, 4, NDC, 128], BF16, 4 * NDC * 128 * 2)
    silu_sb = ov("silu_sb", [128, 2, 512], F32, 2 * 512 * 4)
    assert o[0] - OV <= avail, (o[0] - OV, avail)
    xfin = at("xfin", [128, 2, D], F32, w13_off)

    PS = nc.alloc_psum_tensor("PS", [128, 8, 512], F32)

    def emit(P: Prog):
        t, v, s, g, sp = P.t, P.v, P.s, P.g, P.sp
        c = P.c

        g.dma_start(out=onec_sb[:], in_=ones_col).inc("sw", 16)
        g.dma_start(out=oner_sb[:], in_=ones_row).inc("sw", 16)
        consts_loaded = c["sw"]

        xslot = ["sxa", "sxb"]
        sq_done, stat_mm = [], []
        t.wait("sw", consts_loaded)
        for cn in range(NDC):
            sl = cn % 2
            if cn >= 2:
                g.wait("smm", stat_mm[cn - 2])
                g.wait("sscl", sq_done[cn - 2])
            g.dma_start(out=xroll[:, sl],
                        in_=x_fm[cn].rearrange("p (a b) -> p a b", a=4)).inc(xslot[sl], 16)

            s.wait(xslot[sl], 16 * (cn // 2 + 1))
            if cn >= 2:
                s.wait("smm", stat_mm[cn - 2])
            s.activation(out=sqb[:, sl].rearrange("p a b -> p (a b)"),
                         in_=xroll[:, sl].rearrange("p a b -> p (a b)"),
                         func=AF.Square)
            s.dinc("sscl", 1)
            sq_done.append(c["sscl"])

            t.wait(xslot[sl], 16 * (cn // 2 + 1))
            for tt in range(4):
                t.matmul(PS[0:1, tt, :], onec_sb[:], xroll[:, sl, tt, :],
                         start=(cn == 0), stop=(cn == NDC - 1))
            t.wait("sscl", sq_done[cn])
            for tt in range(4):
                mm = t.matmul(PS[0:1, 4 + tt, :], onec_sb[:], sqb[:, sl, tt, :],
                              start=(cn == 0), stop=(cn == NDC - 1))
            mm.inc("smm", 1)
            stat_mm.append(c["smm"])

        v.wait("smm", c["smm"])
        v.tensor_copy(sumrow[:], PS[0:1, 0:4, :].rearrange("p a b -> p (a b)"))
        v.tensor_copy(sqrow[:], PS[0:1, 4:8, :].rearrange("p a b -> p (a b)"))
        v.drain()
        v.tensor_scalar_mul(mrow[:], sumrow[:], 1.0 / D)
        v.tensor_scalar_mul(sqrow[:], sqrow[:], 1.0 / D)
        v.drain()
        v.tensor_mul(sumrow[:], mrow[:], mrow[:])
        v.drain()
        v.tensor_sub(varrow[:], sqrow[:], sumrow[:])
        v.drain()
        v.tensor_scalar_add(varrow[:], varrow[:], EPS)
        v.dinc("svec", 1)
        s.wait("svec", c["svec"])
        s.activation(out=srow[:], in_=varrow[:], func=AF.Sqrt)
        s.dinc("sscl", 1)
        v.wait("sscl", c["sscl"])
        v.reciprocal(rrow[:], srow[:])
        v.drain()
        v.tensor_copy(mrow_bf[:], mrow[:])
        v.tensor_copy(rrow_bf[:], rrow[:])
        v.dinc("svec", 1)
        rows_done = c["svec"]

        t.wait("svec", rows_done)
        for tt in range(4):
            t.matmul(PS[:, tt, :], oner_sb[:], mrow_bf[0:1, tt * 512:(tt + 1) * 512],
                     start=True, stop=True)
            mm = t.matmul(PS[:, 4 + tt, :], oner_sb[:], rrow_bf[0:1, tt * 512:(tt + 1) * 512],
                          start=True, stop=True)
        mm.inc("smm", 1)
        bcast_mm = c["smm"]

        norm_done = []
        for cn in range(NDC):
            sl = cn % 2
            g.wait("smm", stat_mm[NDC - 1])
            g.wait("sscl", sq_done[NDC - 1])
            if cn >= 2:
                g.wait("svec", norm_done[cn - 2])
            g.dma_start(out=xroll[:, sl],
                        in_=x_fm[cn].rearrange("p (a b) -> p a b", a=4)).inc(xslot[sl], 16)

            v.wait(xslot[sl], 16 * (NDC // 2 + cn // 2 + 1))
            v.wait("smm", bcast_mm)
            v.tensor_sub(tmpn[:], xroll[:, sl], PS[:, 0:4, :])
            v.drain()
            v.tensor_mul(xn[:, cn], tmpn[:], PS[:, 4:8, :])
            v.dinc("svec", 1)
            norm_done.append(c["svec"])
        xn_done = c["svec"]

        g.wait("svec", xn_done)
        g.dma_start(out=ident_sb[:], in_=ident).inc("sw", 16)
        g.dma_start(out=wqkv_sb[:], in_=wqkv).inc("sw", 16)
        g.dma_start(out=wout_sb[:], in_=wout).inc("sw", 16)
        g.dma_start(out=gatew_sb[:], in_=gatew).inc("sw", 16)
        g.dma_start(out=gqb_sb[:], in_=gqb).inc("sw", 16)
        g.dma_start(out=bqb_sb[:], in_=bqb).inc("sw", 16)
        g.dma_start(out=gkb_sb[:], in_=gkb).inc("sw", 16)
        g.dma_start(out=bkb_sb[:], in_=bkb).inc("sw", 16)
        g.dma_start(out=boutb_sb[:], in_=boutb).inc("sw", 16)
        p0b_w = c["sw"]

        if debug:
            for cn in range(NDC):
                sp.wait("svec", xn_done)
                sp.dma_start(out=dbg["xn"][cn],
                             in_=xn[:, cn].rearrange("p a b -> p (a b)")).inc("sy", 16)

        def head_ln(bank, mm_cnt, gb, bb, out_bf):
            v.wait("smm", mm_cnt)
            v.reduce_sum(sm1[:, 0:8].rearrange("p (g w) -> p g w", w=1),
                         PS[:, bank, :].rearrange("p (g w) -> p g w", g=8), axis=AX.X)
            v.drain()
            v.tensor_scalar_mul(sm1[:, 0:8], sm1[:, 0:8], 1.0 / DH)
            v.dinc("svec", 1)
            s.wait("smm", mm_cnt)
            s.wait("svec", c["svec"])
            for h in range(8):
                s.activation(out=qtmp[:, h * 64:(h + 1) * 64],
                             in_=PS[:, bank, h * 64:(h + 1) * 64],
                             func=AF.Square, accum_out=sm2[:, h:h + 1])
            s.dinc("sscl", 1)
            v.wait("sscl", c["sscl"])
            v.tensor_scalar_mul(sm3[:, 0:8], sm2[:, 0:8], 1.0 / DH)
            v.tensor_mul(sm4[:, 0:8], sm1[:, 0:8], sm1[:, 0:8])
            v.drain()
            v.tensor_sub(sm3[:, 0:8], sm3[:, 0:8], sm4[:, 0:8])
            v.drain()
            v.tensor_scalar_add(sm3[:, 0:8], sm3[:, 0:8], EPS)
            v.dinc("svec", 1)
            s.wait("svec", c["svec"])
            s.activation(out=sm4[:, 0:8], in_=sm3[:, 0:8], func=AF.Sqrt)
            s.dinc("sscl", 1)
            v.wait("sscl", c["sscl"])
            v.reciprocal(sm5[:, 0:8], sm4[:, 0:8])
            v.drain()
            for h in range(8):
                v.tensor_scalar(qtmp[:, h * 64:(h + 1) * 64], PS[:, bank, h * 64:(h + 1) * 64],
                                sm1[:, h:h + 1], sm5[:, h:h + 1], ALU.subtract, ALU.mult)
            v.drain()
            v.tensor_mul(qtmp[:], qtmp[:], gb[:])
            v.drain()
            v.tensor_add(out_bf, qtmp[:], bb[:])
            v.dinc("svec", 1)

        t.wait("sw", p0b_w)
        qkv_evac = []
        kv_last_mm = 0
        for tc in range(NTC):
            ts128 = slice(tc * 128, (tc + 1) * 128)
            xn_tile = lambda dc: xn[:, dc, tc // 4, (tc % 4) * 128:(tc % 4) * 128 + 128]
            if tc == 0:
                t.wait("svec", xn_done)
            else:
                t.wait("svec", qkv_evac[tc - 1])
            for bank, fo0 in ((0, INNER), (1, 2 * INNER), (2, 0)):
                for dc in range(NDC):
                    mm = t.matmul(PS[:, bank, :], xn_tile(dc), wqkv_sb[:, dc, fo0:fo0 + 512],
                                  start=(dc == 0), stop=(dc == NDC - 1))
                mm.inc("smm", 1)
            for dc in range(NDC):
                mm = t.matmul(PS[:, 3, 0:NE], xn_tile(dc), gatew_sb[:, dc, :],
                              start=(dc == 0), stop=(dc == NDC - 1))
            mm.inc("smm", 1)
            qkv_mm = c["smm"]

            head_ln(0, qkv_mm - 3, gkb_sb, bkb_sb, klnb[:])
            v.wait("smm", qkv_mm - 2)
            v.tensor_copy(vb[:], PS[:, 1, :])
            v.dinc("svec", 1)
            t.wait("svec", c["svec"])
            for h in range(8):
                mm = t.matmul(PS[0:64, 4, h * 64:(h + 1) * 64], klnb[:, h * 64:(h + 1) * 64],
                              vb[:, h * 64:(h + 1) * 64],
                              start=(tc == 0 and h == 0),
                              stop=(tc == NTC - 1 and h == 7))
            mm.inc("smm", 1)
            kv_last_mm = c["smm"]

            head_ln(2, qkv_mm - 1, gqb_sb, bqb_sb,
                    qln_tm[:].rearrange("p a b -> p (a b)"))
            for fc in range(4):
                t.wait("svec", c["svec"])
                t.transpose(PS[:, 5, 0:64].bitcast(BF16), qln_tm[:, fc, :],
                            ident_sb[:]).inc("smm", 1)
                v.wait("smm", c["smm"])
                v.tensor_copy(qlnT[:, fc, ts128], PS[:, 5, 0:64].bitcast(BF16))
                v.dinc("svec", 1)

            v.wait("smm", qkv_mm)
            v.tensor_copy(scores[:, tc, :], PS[:, 3, 0:NE])
            v.dinc("svec", 1)
            qkv_evac.append(c["svec"])

        def emit_comb():
            for tc in range(NTC):
                v.reduce_max(sm1[:, tc:tc + 1], scores[:, tc, :], axis=AX.X)
            v.drain()
            v.tensor_scalar_mul(sm2[:], sm1[:], -1.0)
            v.dinc("svec", 1)
            s.wait("svec", c["svec"])
            for tc in range(NTC):
                s.activation(out=scores2[:, tc, :], in_=scores[:, tc, :], func=AF.Exp,
                             bias=sm2[:, tc:tc + 1], accum_out=sm3[:, tc:tc + 1])
            s.dinc("sscl", 1)
            v.wait("sscl", c["sscl"])
            v.reciprocal(sm5[:], sm3[:])
            v.drain()
            for tc in range(NTC):
                v.tensor_scalar_mul(prob8[:], scores2[:, tc, :], sm5[:, tc:tc + 1])
                v.drain()
                v.tensor_tensor(gs4[:], prob8[:, 0:8:2], prob8[:, 1:8:2], ALU.max)
                v.drain()
                v.tensor_tensor(gsh[:], gs4[:, 0:4:2], gs4[:, 1:4:2], ALU.max)
                v.tensor_tensor(gsl[:], gs4[:, 0:4:2], gs4[:, 1:4:2], ALU.min)
                v.drain()
                v.tensor_tensor(sm6[:, 0:1], gsh[:, 0:1], gsh[:, 1:2], ALU.min)
                v.tensor_tensor(sm6[:, 1:2], gsl[:, 0:1], gsl[:, 1:2], ALU.max)
                v.drain()
                v.tensor_tensor(sm6[:, 2:3], sm6[:, 0:1], sm6[:, 1:2], ALU.max)
                v.drain()
                v.tensor_single_scalar(gs4[:], gs4[:], sm6[:, 2:3], ALU.is_ge)
                v.drain()
                v.tensor_mul(msk8[:, 0:8:2], prob8[:, 0:8:2], gs4[:])
                v.tensor_mul(msk8[:, 1:8:2], prob8[:, 1:8:2], gs4[:])
                v.drain()
                v.reduce_max(sm6[:, 3:4], msk8[:], axis=AX.X)
                v.drain()
                v.tensor_single_scalar(msk8b[:], msk8[:], sm6[:, 3:4], ALU.is_equal)
                v.drain()
                v.tensor_scalar(msk8b[:], msk8b[:], -1.0, 1.0, ALU.mult, ALU.add)
                v.drain()
                v.tensor_mul(msk8b[:], msk8[:], msk8b[:])
                v.drain()
                v.reduce_max(sm6[:, 4:5], msk8b[:], axis=AX.X)
                v.drain()
                v.tensor_add(sm6[:, 5:6], sm6[:, 3:4], sm6[:, 4:5])
                v.drain()
                v.reciprocal(sm6[:, 6:7], sm6[:, 5:6])
                v.tensor_single_scalar(msk8b[:], msk8[:], sm6[:, 4:5], ALU.is_ge)
                v.drain()
                v.scalar_tensor_tensor(msk8[:], msk8[:], sm6[:, 6:7], msk8b[:],
                                       ALU.mult, ALU.mult)
                v.drain()
                v.tensor_copy(comb[:, tc, :], msk8[:])
            v.dinc("svec", 1)
            pass


        slot_sems = ["s13a", "s13b", "s13c", "s13d"]
        slot_uses = [0, 0, 0, 0]
        slot_freed_at = [0, 0, 0, 0]

        def issue_w13(e, j):
            for half, sidx in ((0, (2 * j) % 4), (1, (2 * j + 1) % 4)):
                g.wait("smm", slot_freed_at[sidx])
                g.dma_start(out=w13_sb[:, sidx],
                            in_=w13[e, half * NIC + j]).inc(slot_sems[sidx], 16)
                slot_uses[sidx] += 1

        issue_w13(0, 0)
        issue_w13(0, 1)

        v.wait("smm", kv_last_mm)
        v.tensor_copy(kvloc[:], PS[0:64, 4, :])
        v.memset(kvz[:], 0.0)
        v.dinc("svec", 1)
        kv_evac_v = c["svec"]
        cc_state = {}

        def emit_cc():
            g.wait("svec", kv_evac_v)
            g.dma_start(out=cc_in.ap(), in_=kvloc[:]).inc("sw", 16)
            g.wait("sw", c["sw"])
            g.collective_compute(
                "AllReduce", ALU.add,
                replica_groups=[[0, 1, 2, 3], [4, 5, 6, 7]],
                ins=[cc_in.ap()],
                outs=[cc_out.ap()],
            ).inc("scc", 1)
            g.wait("scc", 1)
            for cp in range(4):
                g.dma_start(out=kvz[0:64, cp, 0:64],
                            in_=cc_out.ap()[0:64, (2 * cp) * 64:(2 * cp + 1) * 64]).inc("sw", 16)
                g.dma_start(out=kvz[64:128, cp, 64:128],
                            in_=cc_out.ap()[0:64, (2 * cp + 1) * 64:(2 * cp + 2) * 64]).inc("sw", 16)
            cc_state["ld"] = c["sw"]


        attn_state = {}

        def emit_attention():
            v.wait("sw", cc_state["ld"])
            v.tensor_scalar_mul(kvbd[:].rearrange("p a b -> p (a b)"),
                                kvz[:].rearrange("p a b -> p (a b)"), KV_SCALE)
            v.dinc("svec", 1)
            kvbf_done = c["svec"]
            t.wait("svec", kvbf_done)
            att_base = c["svec"]
            for idx in range(16):
                cp, tt = idx // 4, idx % 4
                bank = 6 + idx % 2
                tsl = slice(tt * 512, (tt + 1) * 512)
                if idx >= 2:
                    t.wait("svec", att_base + idx - 1)
                t.matmul(PS[:, bank, :], kvbd[:, cp, :], qlnT[:, cp, tsl],
                         start=True, stop=True).inc("smm", 1)
                v.wait("smm", c["smm"])
                v.tensor_copy(attn_inner[:, cp, tsl], PS[:, bank, :])
                v.dinc("svec", 1)
            att_evac = c["svec"]

            for tc in range(NTC):
                ts128 = slice(tc * 128, (tc + 1) * 128)
                for ht in range(2):
                    idx = 2 * tc + ht
                    bank = 6 + idx % 2
                    if idx >= 2:
                        t.wait("svec", att_evac + idx - 1)
                    else:
                        t.wait("svec", att_evac)
                    for ic in range(4):
                        mm = t.matmul(PS[:, bank, :], attn_inner[:, ic, ts128],
                                      wout_sb[:, ic, ht * 512:(ht + 1) * 512],
                                      start=(ic == 0), stop=(ic == 3))
                    mm.inc("smm", 1)
                    v.wait("smm", c["smm"])
                    v.tensor_tensor(acc[:, tc, ht * 512:(ht + 1) * 512], PS[:, bank, :],
                                    boutb_sb[:, ht * 512:(ht + 1) * 512], ALU.add)
                    v.dinc("svec", 1)
            attn_state["done_v"] = c["svec"]
            attn_state["done_t"] = c["smm"]

        w2_loads = 0
        p2_last = 0

        def issue_w2_part(e, part):
            nonlocal w2_loads
            if e == 0:
                g.wait("svec", attn_state["done_v"])
            else:
                g.wait("smm", p2_last)
            g.dma_start(out=w2_sb[:, 4 * part:4 * part + 4, :],
                        in_=w2[e][:, 4 * part:4 * part + 4, :]).inc("s2l", 16)
            w2_loads += 1

        unit_done = []
        fslot = ["sfa", "sfb"]
        fin_v = []
        for e in range(NE):
            for j in range(NIC):
                if not (e == 0 and j < 2):
                    issue_w13(e, j)
                if e == 0:
                    if j == 1:
                        emit_comb()
                    if j == 3:
                        emit_cc()
                    if j in (6, 8, 10, 12):
                        issue_w2_part(e, (j - 6) // 2)
                else:
                    if j in (1, 3, 5, 7):
                        issue_w2_part(e, (j - 1) // 2)

                sg, su = (2 * j) % 4, (2 * j + 1) % 4
                t.wait(slot_sems[sg], 16 * slot_uses[sg])
                t.wait(slot_sems[su], 16 * slot_uses[su])
                if e == 0 and j == 0:
                    t.wait("svec", kv_evac_v)
                for tt in range(4):
                    bb = (len(unit_done) % 3) * 2
                    if len(unit_done) >= 3:
                        t.wait("sscl", unit_done[-3][0])
                        t.wait("svec", unit_done[-3][1])
                    for dc in range(NDC):
                        mm = t.matmul(PS[:, bb, :], w13_sb[:, sg, dc, :], xn[:, dc, tt, :],
                                      start=(dc == 0), stop=(dc == NDC - 1))
                    mm.inc("smm", 1)
                    gate_cnt = c["smm"]
                    for dc in range(NDC):
                        mm = t.matmul(PS[:, bb + 1, :], w13_sb[:, su, dc, :], xn[:, dc, tt, :],
                                      start=(dc == 0), stop=(dc == NDC - 1))
                    mm.inc("smm", 1)
                    up_cnt = c["smm"]
                    u = len(unit_done)
                    s.wait("smm", gate_cnt)
                    if u >= 2:
                        s.wait("svec", unit_done[-2][1])
                    s.activation(out=silu_sb[:, u % 2, :], in_=PS[:, bb, :], func=AF.Silu)
                    s.dinc("sscl", 1)
                    v.wait("sscl", c["sscl"])
                    v.wait("smm", up_cnt)
                    v.tensor_mul(act[:, j, tt * 512:(tt + 1) * 512], silu_sb[:, u % 2, :],
                                 PS[:, bb + 1, :])
                    v.dinc("svec", 1)
                    unit_done.append((c["sscl"], c["svec"]))
                slot_freed_at[sg] = up_cnt
                slot_freed_at[su] = up_cnt
                if e == 0 and j == 5:
                    emit_attention()
            act_done = c["svec"]
            act_done_t = c["smm"]

            t.wait("s2l", 16 * w2_loads)
            t.wait("svec", act_done)
            p2_base = c["svec"]
            last = (e == NE - 1)
            fuses = [0, 0]
            for tc in range(NTC):
                ts128 = slice(tc * 128, (tc + 1) * 128)
                if last:
                    sl = tc % 2
                    if tc < 2:
                        sp.wait("smm", act_done_t)
                    else:
                        sp.wait("svec", fin_v[tc - 2])
                    sp.dma_start(out=xfin[:, sl], in_=x_tm[tc]).inc(fslot[sl], 16)
                    fuses[sl] += 1
                for ht in range(2):
                    idx = 2 * tc + ht
                    bank = 6 + idx % 2
                    if idx >= 2:
                        t.wait("svec", p2_base + idx - 1 + (tc - 1 if last else 0))
                    for ic in range(NIC):
                        mm = t.matmul(PS[:, bank, :], act[:, ic, ts128],
                                      w2_sb[:, ic, ht * 512:(ht + 1) * 512],
                                      start=(ic == 0), stop=(ic == NIC - 1))
                    mm.inc("smm", 1)
                    v.wait("smm", c["smm"])
                    v.scalar_tensor_tensor(acc[:, tc, ht * 512:(ht + 1) * 512],
                                           PS[:, bank, :], comb[:, tc, e:e + 1],
                                           acc[:, tc, ht * 512:(ht + 1) * 512],
                                           ALU.mult, ALU.add)
                    v.dinc("svec", 1)
                if last:
                    v.wait(fslot[sl], 16 * fuses[sl])
                    v.tensor_add(acc[:, tc, :], acc[:, tc, :], xfin[:, sl, :])
                    v.dinc("svec", 1)
                    fin_v.append(c["svec"])
                    sp.wait("svec", c["svec"])
                    sp.dma_start(out=y_out[tc], in_=acc[:, tc, :]).inc("sy", 16)
            p2_last = c["smm"]

        ffn_done_v = c["svec"]

        if debug:
            sp.wait("svec", ffn_done_v)
            for tc in range(NTC):
                sp.dma_start(out=dbg["comb"][tc], in_=comb[:, tc, :]).inc("sy", 16)
            for fc in range(4):
                sp.dma_start(out=dbg["qln"][fc], in_=qlnT[:, fc, :]).inc("sy", 16)

        sp.wait("sy", c["sy"])

    with contextlib.ExitStack() as es:
        sems = {nm: es.enter_context(nc.semaphore(nm)) for nm in SEM_NAMES}
        block = es.enter_context(nc.Block())

        @block.tensor
        def _(h):
            emit(Prog(sems, "t", h))

        @block.vector
        def _(h):
            emit(Prog(sems, "v", h))

        @block.scalar
        def _(h):
            emit(Prog(sems, "s", h))

        @block.gpsimd
        def _(h):
            emit(Prog(sems, "g", h))

        @block.sync
        def _(h):
            emit(Prog(sems, "sp", h))

    return nc


def _prep_core_inputs(inputs):
    x = np.asarray(inputs["L0_attn_x"], np.float32).reshape(T, D)
    W_qkv = np.asarray(inputs["L0_attn_W_qkv"], np.float32)
    gq = np.asarray(inputs["L0_attn_gamma_q"], np.float32).reshape(INNER)
    bq = np.asarray(inputs["L0_attn_beta_q"], np.float32).reshape(INNER)
    gk = np.asarray(inputs["L0_attn_gamma_k"], np.float32).reshape(INNER)
    bk = np.asarray(inputs["L0_attn_beta_k"], np.float32).reshape(INNER)
    W_out = np.asarray(inputs["L0_attn_W_out"], np.float32)
    b_out = np.asarray(inputs["L0_attn_b_out"], np.float32)
    gate_w = np.asarray(inputs["L0_ffn_gate_w"], np.float32)
    w13 = np.asarray(inputs["L0_ffn_w13_weight"], np.float32)
    w2 = np.asarray(inputs["L0_ffn_w2_weight"], np.float32)

    wqkv_h = np.ascontiguousarray(W_qkv.T.reshape(NDC, 128, 3 * INNER).transpose(1, 0, 2))
    wout_h = np.ascontiguousarray(W_out.T.reshape(4, 128, D).transpose(1, 0, 2))
    gatew_h = np.ascontiguousarray(gate_w.T.reshape(NDC, 128, NE).transpose(1, 0, 2))
    w13_h = np.ascontiguousarray(
        w13.transpose(0, 2, 1).reshape(NE, NDC, 128, 2 * NIC, 128).transpose(0, 3, 2, 1, 4))
    w2_h = np.ascontiguousarray(
        w2.transpose(0, 2, 1).reshape(NE, NIC, 128, D).transpose(0, 2, 1, 3))
    gqb_h = np.ascontiguousarray(np.broadcast_to(gq, (128, INNER)))
    bqb_h = np.ascontiguousarray(np.broadcast_to(bq, (128, INNER)))
    gkb_h = np.ascontiguousarray(np.broadcast_to(gk, (128, INNER)))
    bkb_h = np.ascontiguousarray(np.broadcast_to(bk, (128, INNER)))
    boutb_h = np.ascontiguousarray(np.broadcast_to(b_out, (128, D)))
    ident_h = np.eye(128, dtype=BF)
    ones_col_h = np.ones((128, 1), BF)
    ones_row_h = np.ones((1, 128), BF)

    in_maps = []
    for cid in range(NCORES):
        sl = x[cid * TPC:(cid + 1) * TPC]
        in_maps.append({
            "x_fm": np.ascontiguousarray(sl.T).reshape(NDC, 128, TPC),
            "x_tm": np.ascontiguousarray(sl).reshape(NTC, 128, D),
            "wqkv": wqkv_h, "wout": wout_h, "gatew": gatew_h,
            "w13": w13_h, "w2": w2_h,
            "gqb": gqb_h, "bqb": bqb_h, "gkb": gkb_h, "bkb": bkb_h,
            "boutb": boutb_h, "ident": ident_h,
            "ones_col": ones_col_h, "ones_row": ones_row_h,
        })
    return in_maps


_NC_CACHE = {}


def kernel(**inputs):
    if "nc" not in _NC_CACHE:
        _NC_CACHE["nc"] = build_nc(debug=False)
    nc = _NC_CACHE["nc"]
    in_maps = _prep_core_inputs(inputs)
    res = run_bass_kernel_spmd(nc, in_maps, core_ids=list(range(NCORES)))
    out = np.empty((T, D), np.float32)
    for cid in range(NCORES):
        out[cid * TPC:(cid + 1) * TPC] = res.results[cid]["y"].reshape(TPC, D)
    return out.reshape(B, N, D)


# version 10
# speedup vs baseline: 1.0682x; 1.0410x over previous
import contextlib
import numpy as np
import ml_dtypes
import concourse.bass as bass
import concourse.mybir as mybir
from concourse.bass_utils import run_bass_kernel_spmd

F32 = mybir.dt.float32
BF16 = mybir.dt.bfloat16
AF = mybir.ActivationFunctionType
ALU = mybir.AluOpType
AX = mybir.AxisListType

B, N, D, H, DH = 2, 8192, 1024, 8, 64
T = B * N
NCORES = 8
TPC = T // NCORES
NTC = TPC // 128
NDC = D // 128
INNER = H * DH
NE = 8
INTER = 2048
NIC = INTER // 128
EPS = 1e-5
KV_SCALE = 1.0 / (np.sqrt(float(DH)) * float(N))

BF = np.dtype(ml_dtypes.bfloat16)

SEM_NAMES = ["sw", "sxa", "sxb", "sfa", "sfb",
             "s13a", "s13b", "s13c", "s13d", "s2l",
             "smm", "svec", "sscl", "scc", "sy"]


class _InstP:
    __slots__ = ("p", "inst")

    def __init__(self, p, inst):
        self.p = p
        self.inst = inst

    def inc(self, sem, n=1):
        self.p.c[sem] = self.p.c.get(sem, 0) + n
        if self.inst is not None:
            self.inst.then_inc(self.p.sems[sem], n)
        return self


class _EngP:
    __slots__ = ("p", "name")

    def __init__(self, p, name):
        self.p = p
        self.name = name

    def wait(self, sem, val):
        if val <= 0:
            return
        if self.p.active == self.name:
            self.p.h.wait_ge(self.p.sems[sem], val)

    def dinc(self, sem, n=1):
        self.p.c[sem] = self.p.c.get(sem, 0) + n
        if self.p.active == self.name:
            self.p.h.drain().then_inc(self.p.sems[sem], n)

    def __getattr__(self, meth):
        p, name = self.p, self.name

        def call(*a, **kw):
            inst = None
            if p.active == name:
                inst = getattr(p.h, meth)(*a, **kw)
            return _InstP(p, inst)

        return call


class Prog:
    def __init__(self, sems, active, handle):
        self.sems = sems
        self.active = active
        self.h = handle
        self.c = {k: 0 for k in SEM_NAMES}
        self.t = _EngP(self, "t")
        self.v = _EngP(self, "v")
        self.s = _EngP(self, "s")
        self.g = _EngP(self, "g")
        self.sp = _EngP(self, "sp")


def build_nc(debug=False):
    nc = bass.Bass("TRN2", target_bir_lowering=False, num_devices=NCORES,
                   detect_race_conditions=False)

    def dI(name, shape, dt=F32):
        return nc.dram_tensor(name, shape, dt, kind="ExternalInput").ap()

    x_fm = dI("x_fm", [NDC, 128, TPC])
    x_tm = dI("x_tm", [NTC, 128, D])
    wqkv = dI("wqkv", [128, NDC, 3 * INNER])
    wout = dI("wout", [128, 4, D])
    gatew = dI("gatew", [128, NDC, NE])
    w13 = dI("w13", [NE, 2 * NIC, 128, NDC, 128])
    w2 = dI("w2", [NE, 128, NIC, D])
    gqb = dI("gqb", [128, INNER])
    bqb = dI("bqb", [128, INNER])
    gkb = dI("gkb", [128, INNER])
    bkb = dI("bkb", [128, INNER])
    boutb = dI("boutb", [128, D])
    ident = dI("ident", [128, 128], BF16)
    ones_col = dI("ones_col", [128, 1], BF16)
    ones_row = dI("ones_row", [1, 128], BF16)

    y_out = nc.dram_tensor("y", [NTC, 128, D], F32, kind="ExternalOutput").ap()
    cc_in = nc.dram_tensor("cc_in", [64, H * DH], F32, kind="Internal")
    cc_out = nc.dram_tensor("cc_out", [64, H * DH], F32, kind="Internal")

    dbg = {}
    if debug:
        dbg["xn"] = nc.dram_tensor("dbg_xn", [NDC, 128, TPC], BF16, kind="ExternalOutput").ap()
        dbg["qln"] = nc.dram_tensor("dbg_qln", [4, 128, TPC], BF16, kind="ExternalOutput").ap()
        dbg["comb"] = nc.dram_tensor("dbg_comb", [NTC, 128, NE], F32, kind="ExternalOutput").ap()
        dbg["kv"] = nc.dram_tensor("dbg_kv", [64, H * DH], F32, kind="ExternalOutput").ap()
        dbg["acc_attn"] = nc.dram_tensor("dbg_acc_attn", [NTC, 128, D], F32, kind="ExternalOutput").ap()

    base = ((nc.sbuf_base + 31) // 32) * 32
    top = nc.sbuf_top

    def at(name, shape, dt, off):
        return nc.alloc_sbuf_tensor_at(name, shape, dt, offset=off)

    cur = base

    def persist(name, shape, dt, nbytes):
        nonlocal cur
        t = at(name, shape, dt, cur)
        cur += nbytes
        return t

    xn = persist("xn", [128, NDC, 4, 512], BF16, NDC * 2048 * 2)
    acc = persist("acc", [128, NTC, D], F32, NTC * D * 4)
    comb = persist("comb", [128, NTC, NE], F32, NTC * NE * 4)
    OV = cur
    avail = top - 64 - OV

    o = [OV]

    def ov(name, shape, dt, nbytes):
        t = at(name, shape, dt, o[0])
        o[0] += (nbytes + 31) // 32 * 32
        return t

    xroll = ov("xroll", [128, 2, 4, 512], BF16, 2 * 2048 * 2)
    sqb = ov("sqb", [128, 2, 4, 512], BF16, 2 * 2048 * 2)
    mrow_bf = ov("mrow_bf", [1, 2048], BF16, 2048 * 2)
    rrow_bf = ov("rrow_bf", [1, 2048], BF16, 2048 * 2)
    mbc = ov("mbc", [128, 4, 512], F32, 2048 * 4)
    rbc = ov("rbc", [128, 4, 512], F32, 2048 * 4)
    onec_sb = ov("onec_sb", [128, 1], BF16, 32)
    oner_sb = ov("oner_sb", [1, 128], BF16, 256)
    tmpn = ov("tmpn", [128, 4, 512], F32, 2048 * 4)
    assert o[0] - OV <= avail, (o[0] - OV, avail)

    o = [OV]
    wqkv_sb = ov("wqkv_sb", [128, NDC, 3 * INNER], BF16, NDC * 3 * INNER * 2)
    wout_sb = ov("wout_sb", [128, 4, D], BF16, 4 * D * 2)
    gatew_sb = ov("gatew_sb", [128, NDC, NE], BF16, 256)
    qlnT = ov("qlnT", [128, 4, TPC], BF16, 4 * TPC * 2)
    attn_inner = ov("attn_inner", [128, 4, TPC], BF16, 4 * TPC * 2)
    qln_tm = ov("qln_tm", [128, 4, 128], BF16, 1024)
    qtmp = ov("qtmp", [128, 512], F32, 2048)
    klnb = ov("klnb", [128, 512], BF16, 1024)
    vb = ov("vb", [128, 512], BF16, 1024)
    kvloc = ov("kvloc", [64, 512], F32, 2048)
    kvz = ov("kvz", [128, 4, 128], F32, 2048)
    kvbd = ov("kvbd", [128, 4, 128], BF16, 1024)
    scores = ov("scores", [128, NTC, NE], F32, NTC * NE * 4)
    scores2 = ov("scores2", [128, NTC, NE], F32, NTC * NE * 4)
    prob8 = ov("prob8", [128, NE], F32, 32)
    sm1 = ov("sm1", [128, 16], F32, 64)
    sm2 = ov("sm2", [128, 16], F32, 64)
    sm3 = ov("sm3", [128, 16], F32, 64)
    sm4 = ov("sm4", [128, 16], F32, 64)
    sm5 = ov("sm5", [128, 16], F32, 64)
    sm6 = ov("sm6", [128, 16], F32, 64)
    gs4 = ov("gs4", [128, 4], F32, 16)
    gs4b = ov("gs4b", [128, 4], F32, 16)
    gsh = ov("gsh", [128, 2], F32, 8)
    gsl = ov("gsl", [128, 2], F32, 8)
    msk8 = ov("msk8", [128, NE], F32, 32)
    msk8b = ov("msk8b", [128, NE], F32, 32)
    ktmp = ov("ktmp", [128, 512], F32, 2048)
    sq1 = ov("sq1", [128, 512], F32, 2048)
    sq2 = ov("sq2", [128, 512], F32, 2048)
    msk16 = ov("msk16", [128, NTC, NE], F32, NTC * NE * 4)
    eq16 = ov("eq16", [128, NTC, NE], F32, NTC * NE * 4)
    gs4a = ov("gs4a", [128, NTC, 4], F32, NTC * 4 * 4)
    gsh2 = ov("gsh2", [128, NTC, 2], F32, NTC * 2 * 4)
    gsl2 = ov("gsl2", [128, NTC, 2], F32, NTC * 2 * 4)
    ident_sb = ov("ident_sb", [128, 128], BF16, 256)
    gqb_sb = ov("gqb_sb", [128, INNER], F32, INNER * 4)
    bqb_sb = ov("bqb_sb", [128, INNER], F32, INNER * 4)
    gkb_sb = ov("gkb_sb", [128, INNER], F32, INNER * 4)
    bkb_sb = ov("bkb_sb", [128, INNER], F32, INNER * 4)
    boutb_sb = ov("boutb_sb", [128, D], F32, D * 4)
    assert o[0] - OV <= avail, (o[0] - OV, avail)

    o = [OV]
    act = ov("act", [128, NIC, TPC], BF16, NIC * TPC * 2)
    w2_sb = ov("w2_sb", [128, NIC, D], BF16, NIC * D * 2)
    w13_off = o[0]
    w13_sb = ov("w13_sb", [128, 4, NDC, 128], BF16, 4 * NDC * 128 * 2)
    silu_sb = ov("silu_sb", [128, 2, 512], F32, 2 * 512 * 4)
    assert o[0] - OV <= avail, (o[0] - OV, avail)
    xfin = at("xfin", [128, 2, D], F32, w13_off)

    PS = nc.alloc_psum_tensor("PS", [128, 8, 512], F32)

    def emit(P: Prog):
        t, v, s, g, sp = P.t, P.v, P.s, P.g, P.sp
        c = P.c

        g.dma_start(out=onec_sb[:], in_=ones_col).inc("sw", 16)
        g.dma_start(out=oner_sb[:], in_=ones_row).inc("sw", 16)
        consts_loaded = c["sw"]

        xslot = ["sxa", "sxb"]
        sq_done, stat_mm = [], []
        t.wait("sw", consts_loaded)
        for cn in range(NDC):
            sl = cn % 2
            if cn >= 2:
                g.wait("smm", stat_mm[cn - 2])
                g.wait("sscl", sq_done[cn - 2])
            g.dma_start(out=xroll[:, sl],
                        in_=x_fm[cn].rearrange("p (a b) -> p a b", a=4)).inc(xslot[sl], 16)

            s.wait(xslot[sl], 16 * (cn // 2 + 1))
            if cn >= 2:
                s.wait("smm", stat_mm[cn - 2])
            s.activation(out=sqb[:, sl].rearrange("p a b -> p (a b)"),
                         in_=xroll[:, sl].rearrange("p a b -> p (a b)"),
                         func=AF.Square)
            s.dinc("sscl", 1)
            sq_done.append(c["sscl"])

            t.wait(xslot[sl], 16 * (cn // 2 + 1))
            for tt in range(4):
                t.matmul(PS[0:1, tt, :], onec_sb[:], xroll[:, sl, tt, :],
                         start=(cn == 0), stop=(cn == NDC - 1))
            t.wait("sscl", sq_done[cn])
            for tt in range(4):
                mm = t.matmul(PS[0:1, 4 + tt, :], onec_sb[:], sqb[:, sl, tt, :],
                              start=(cn == 0), stop=(cn == NDC - 1))
            mm.inc("smm", 1)
            stat_mm.append(c["smm"])

        v.wait("smm", c["smm"])
        v.tensor_copy(mrow_bf[:], PS[0:1, 0:4, :].rearrange("p a b -> p (a b)"))
        v.tensor_copy(rrow_bf[:], PS[0:1, 4:8, :].rearrange("p a b -> p (a b)"))
        v.dinc("svec", 1)
        rows_done = c["svec"]
        t.wait("svec", rows_done)
        for tt in range(4):
            t.matmul(PS[:, tt, :], oner_sb[:], mrow_bf[0:1, tt * 512:(tt + 1) * 512],
                     start=True, stop=True)
            mm = t.matmul(PS[:, 4 + tt, :], oner_sb[:], rrow_bf[0:1, tt * 512:(tt + 1) * 512],
                          start=True, stop=True)
        mm.inc("smm", 1)
        bcast_mm = c["smm"]
        v.wait("smm", bcast_mm)
        v.tensor_scalar_mul(mbc[:], PS[:, 0:4, :], 1.0 / D)
        v.tensor_scalar_mul(rbc[:], PS[:, 4:8, :], 1.0 / D)
        v.drain()
        v.tensor_mul(tmpn[:], mbc[:], mbc[:])
        v.drain()
        v.tensor_sub(rbc[:], rbc[:], tmpn[:])
        v.drain()
        v.tensor_scalar_add(rbc[:], rbc[:], EPS)
        v.dinc("svec", 1)
        s.wait("svec", c["svec"])
        s.activation(out=rbc[:].rearrange("p a b -> p (a b)"),
                     in_=rbc[:].rearrange("p a b -> p (a b)"), func=AF.Sqrt)
        s.dinc("sscl", 1)
        v.wait("sscl", c["sscl"])
        v.reciprocal(rbc[:].rearrange("p a b -> p (a b)"),
                     rbc[:].rearrange("p a b -> p (a b)"))
        v.dinc("svec", 1)
        stats_v = c["svec"]

        norm_done = []
        v.wait("svec", stats_v)
        for cn in range(NDC):
            sl = cn % 2
            g.wait("smm", stat_mm[NDC - 1])
            g.wait("sscl", sq_done[NDC - 1])
            if cn >= 2:
                g.wait("svec", norm_done[cn - 2])
            g.dma_start(out=xroll[:, sl],
                        in_=x_fm[cn].rearrange("p (a b) -> p a b", a=4)).inc(xslot[sl], 16)

            v.wait(xslot[sl], 16 * (NDC // 2 + cn // 2 + 1))
            v.tensor_sub(tmpn[:], xroll[:, sl], mbc[:])
            v.drain()
            v.tensor_mul(xn[:, cn], tmpn[:], rbc[:])
            v.dinc("svec", 1)
            norm_done.append(c["svec"])
        xn_done = c["svec"]

        g.wait("svec", xn_done)
        g.dma_start(out=ident_sb[:], in_=ident).inc("sw", 16)
        g.dma_start(out=wqkv_sb[:], in_=wqkv).inc("sw", 16)
        g.dma_start(out=wout_sb[:], in_=wout).inc("sw", 16)
        g.dma_start(out=gatew_sb[:], in_=gatew).inc("sw", 16)
        g.dma_start(out=gqb_sb[:], in_=gqb).inc("sw", 16)
        g.dma_start(out=bqb_sb[:], in_=bqb).inc("sw", 16)
        g.dma_start(out=gkb_sb[:], in_=gkb).inc("sw", 16)
        g.dma_start(out=bkb_sb[:], in_=bkb).inc("sw", 16)
        g.dma_start(out=boutb_sb[:], in_=boutb).inc("sw", 16)
        p0b_w = c["sw"]

        if debug:
            for cn in range(NDC):
                sp.wait("svec", xn_done)
                sp.dma_start(out=dbg["xn"][cn],
                             in_=xn[:, cn].rearrange("p a b -> p (a b)")).inc("sy", 16)

        def head_ln_pair(mm_cnt):
            seg = lambda ap: ap.rearrange("p (g w) -> p g w", g=8)
            seg1 = lambda ap: ap.rearrange("p (g w) -> p g w", w=1)
            bc64 = lambda ap: seg1(ap).broadcast_to([128, 8, 64])
            v.wait("smm", mm_cnt)
            v.tensor_copy(vb[:], PS[:, 1, :])
            v.reduce_sum(seg1(sm1[:, 0:8]), seg(PS[:, 0, :]), axis=AX.X)
            v.reduce_sum(seg1(sm1[:, 8:16]), seg(PS[:, 2, :]), axis=AX.X)
            v.drain()
            v.tensor_scalar_mul(sm1[:], sm1[:], 1.0 / DH)
            v.dinc("svec", 1)
            v.tensor_tensor(seg(ktmp[:]), seg(PS[:, 0, :]), bc64(sm1[:, 0:8]), ALU.subtract)
            v.tensor_tensor(seg(qtmp[:]), seg(PS[:, 2, :]), bc64(sm1[:, 8:16]), ALU.subtract)
            v.drain()
            v.tensor_mul(sq1[:], ktmp[:], ktmp[:])
            v.tensor_mul(sq2[:], qtmp[:], qtmp[:])
            v.drain()
            v.reduce_sum(seg1(sm3[:, 0:8]), seg(sq1[:]), axis=AX.X)
            v.reduce_sum(seg1(sm3[:, 8:16]), seg(sq2[:]), axis=AX.X)
            v.drain()
            v.tensor_scalar(sm3[:], sm3[:], 1.0 / DH, EPS, ALU.mult, ALU.add)
            v.dinc("svec", 1)
            s.wait("svec", c["svec"])
            s.activation(out=sm4[:], in_=sm3[:], func=AF.Sqrt)
            s.dinc("sscl", 1)
            v.wait("sscl", c["sscl"])
            v.reciprocal(sm5[:], sm4[:])
            v.drain()
            v.tensor_tensor(seg(ktmp[:]), seg(ktmp[:]), bc64(sm5[:, 0:8]), ALU.mult)
            v.tensor_tensor(seg(qtmp[:]), seg(qtmp[:]), bc64(sm5[:, 8:16]), ALU.mult)
            v.drain()
            v.tensor_mul(ktmp[:], ktmp[:], gkb_sb[:])
            v.tensor_mul(qtmp[:], qtmp[:], gqb_sb[:])
            v.drain()
            v.tensor_add(klnb[:], ktmp[:], bkb_sb[:])
            v.tensor_add(qln_tm[:].rearrange("p a b -> p (a b)"), qtmp[:], bqb_sb[:])
            v.dinc("svec", 1)

        t.wait("sw", p0b_w)
        qkv_evac = []
        kv_last_mm = 0
        for tc in range(NTC):
            ts128 = slice(tc * 128, (tc + 1) * 128)
            xn_tile = lambda dc: xn[:, dc, tc // 4, (tc % 4) * 128:(tc % 4) * 128 + 128]
            if tc == 0:
                t.wait("svec", xn_done)
            else:
                t.wait("svec", qkv_evac[tc - 1])
            for bank, fo0 in ((0, INNER), (1, 2 * INNER), (2, 0)):
                for dc in range(NDC):
                    mm = t.matmul(PS[:, bank, :], xn_tile(dc), wqkv_sb[:, dc, fo0:fo0 + 512],
                                  start=(dc == 0), stop=(dc == NDC - 1))
                mm.inc("smm", 1)
            for dc in range(NDC):
                mm = t.matmul(PS[:, 3, 0:NE], xn_tile(dc), gatew_sb[:, dc, :],
                              start=(dc == 0), stop=(dc == NDC - 1))
            mm.inc("smm", 1)
            qkv_mm = c["smm"]

            head_ln_pair(qkv_mm)
            t.wait("svec", c["svec"])
            for h in range(8):
                mm = t.matmul(PS[0:64, 4, h * 64:(h + 1) * 64], klnb[:, h * 64:(h + 1) * 64],
                              vb[:, h * 64:(h + 1) * 64],
                              start=(tc == 0 and h == 0),
                              stop=(tc == NTC - 1 and h == 7))
            mm.inc("smm", 1)
            kv_last_mm = c["smm"]

            for fc in range(4):
                t.wait("svec", c["svec"])
                t.transpose(PS[:, 5, 0:64].bitcast(BF16), qln_tm[:, fc, :],
                            ident_sb[:]).inc("smm", 1)
                v.wait("smm", c["smm"])
                v.tensor_copy(qlnT[:, fc, ts128], PS[:, 5, 0:64].bitcast(BF16))
                v.dinc("svec", 1)

            v.wait("smm", qkv_mm)
            v.tensor_copy(scores[:, tc, :], PS[:, 3, 0:NE])
            v.dinc("svec", 1)
            qkv_evac.append(c["svec"])

        def emit_comb():
            b1 = lambda ap, w: ap.broadcast_to([128, NTC, w])
            v.reduce_max(sm1[:].rearrange("p (a b) -> p a b", b=1),
                         scores[:], axis=AX.X)
            v.drain()
            v.tensor_tensor(scores[:], scores[:],
                            b1(sm1[:].rearrange("p (a b) -> p a b", b=1), NE),
                            ALU.subtract)
            v.dinc("svec", 1)
            s.wait("svec", c["svec"])
            s.activation(out=scores2[:].rearrange("p a b -> p (a b)"),
                         in_=scores[:].rearrange("p a b -> p (a b)"), func=AF.Exp)
            s.dinc("sscl", 1)
            v.wait("sscl", c["sscl"])
            v.reduce_sum(sm3[:].rearrange("p (a b) -> p a b", b=1), scores2[:], axis=AX.X)
            v.drain()
            v.reciprocal(sm5[:], sm3[:])
            v.drain()
            v.tensor_tensor(scores[:], scores2[:],
                            b1(sm5[:].rearrange("p (a b) -> p a b", b=1), NE), ALU.mult)
            v.drain()
            v.tensor_tensor(gs4a[:], scores[:, :, 0:8:2], scores[:, :, 1:8:2], ALU.max)
            v.drain()
            v.tensor_tensor(gsh2[:], gs4a[:, :, 0:4:2], gs4a[:, :, 1:4:2], ALU.max)
            v.tensor_tensor(gsl2[:], gs4a[:, :, 0:4:2], gs4a[:, :, 1:4:2], ALU.min)
            v.drain()
            v.tensor_tensor(sm1[:].rearrange("p (a b) -> p a b", b=1),
                            gsh2[:, :, 0:1], gsh2[:, :, 1:2], ALU.min)
            v.tensor_tensor(sm2[:].rearrange("p (a b) -> p a b", b=1),
                            gsl2[:, :, 0:1], gsl2[:, :, 1:2], ALU.max)
            v.drain()
            v.tensor_tensor(sm3[:], sm1[:], sm2[:], ALU.max)
            v.drain()
            v.tensor_tensor(gs4a[:], gs4a[:],
                            b1(sm3[:].rearrange("p (a b) -> p a b", b=1), 4), ALU.is_ge)
            v.drain()
            v.tensor_tensor(msk16[:, :, 0:8:2], scores[:, :, 0:8:2], gs4a[:], ALU.mult)
            v.tensor_tensor(msk16[:, :, 1:8:2], scores[:, :, 1:8:2], gs4a[:], ALU.mult)
            v.drain()
            v.reduce_max(sm1[:].rearrange("p (a b) -> p a b", b=1), msk16[:], axis=AX.X)
            v.drain()
            v.tensor_tensor(eq16[:], msk16[:],
                            b1(sm1[:].rearrange("p (a b) -> p a b", b=1), NE), ALU.is_equal)
            v.drain()
            v.tensor_scalar(eq16[:], eq16[:], -1.0, 1.0, ALU.mult, ALU.add)
            v.drain()
            v.tensor_mul(eq16[:], msk16[:], eq16[:])
            v.drain()
            v.reduce_max(sm2[:].rearrange("p (a b) -> p a b", b=1), eq16[:], axis=AX.X)
            v.drain()
            v.tensor_add(sm3[:], sm1[:], sm2[:])
            v.tensor_tensor(eq16[:], msk16[:],
                            b1(sm2[:].rearrange("p (a b) -> p a b", b=1), NE), ALU.is_ge)
            v.drain()
            v.reciprocal(sm5[:], sm3[:])
            v.tensor_mul(msk16[:], msk16[:], eq16[:])
            v.drain()
            v.tensor_tensor(comb[:], msk16[:],
                            b1(sm5[:].rearrange("p (a b) -> p a b", b=1), NE), ALU.mult)
            v.dinc("svec", 1)

        slot_sems = ["s13a", "s13b", "s13c", "s13d"]
        slot_uses = [0, 0, 0, 0]
        slot_freed_at = [0, 0, 0, 0]

        def issue_w13(e, j):
            for half, sidx in ((0, (2 * j) % 4), (1, (2 * j + 1) % 4)):
                g.wait("smm", slot_freed_at[sidx])
                g.dma_start(out=w13_sb[:, sidx],
                            in_=w13[e, half * NIC + j]).inc(slot_sems[sidx], 16)
                slot_uses[sidx] += 1

        issue_w13(0, 0)
        issue_w13(0, 1)

        v.wait("smm", kv_last_mm)
        v.tensor_copy(kvloc[:], PS[0:64, 4, :])
        v.memset(kvz[:], 0.0)
        v.dinc("svec", 1)
        kv_evac_v = c["svec"]
        cc_state = {}

        def emit_cc():
            g.wait("svec", kv_evac_v)
            g.dma_start(out=cc_in.ap(), in_=kvloc[:]).inc("sw", 16)
            g.wait("sw", c["sw"])
            g.collective_compute(
                "AllReduce", ALU.add,
                replica_groups=[[0, 1, 2, 3], [4, 5, 6, 7]],
                ins=[cc_in.ap()],
                outs=[cc_out.ap()],
            ).inc("scc", 1)
            g.wait("scc", 1)
            for cp in range(4):
                g.dma_start(out=kvz[0:64, cp, 0:64],
                            in_=cc_out.ap()[0:64, (2 * cp) * 64:(2 * cp + 1) * 64]).inc("sw", 16)
                g.dma_start(out=kvz[64:128, cp, 64:128],
                            in_=cc_out.ap()[0:64, (2 * cp + 1) * 64:(2 * cp + 2) * 64]).inc("sw", 16)
            cc_state["ld"] = c["sw"]


        attn_state = {}

        def emit_attention():
            v.wait("sw", cc_state["ld"])
            v.tensor_scalar_mul(kvbd[:].rearrange("p a b -> p (a b)"),
                                kvz[:].rearrange("p a b -> p (a b)"), KV_SCALE)
            v.dinc("svec", 1)
            kvbf_done = c["svec"]
            t.wait("svec", kvbf_done)
            att_base = c["svec"]
            for idx in range(16):
                cp, tt = idx // 4, idx % 4
                bank = 6 + idx % 2
                tsl = slice(tt * 512, (tt + 1) * 512)
                if idx >= 2:
                    t.wait("svec", att_base + idx - 1)
                t.matmul(PS[:, bank, :], kvbd[:, cp, :], qlnT[:, cp, tsl],
                         start=True, stop=True).inc("smm", 1)
                v.wait("smm", c["smm"])
                v.tensor_copy(attn_inner[:, cp, tsl], PS[:, bank, :])
                v.dinc("svec", 1)
            att_evac = c["svec"]

            for tc in range(NTC):
                ts128 = slice(tc * 128, (tc + 1) * 128)
                for ht in range(2):
                    idx = 2 * tc + ht
                    bank = 6 + idx % 2
                    if idx >= 2:
                        t.wait("svec", att_evac + idx - 1)
                    else:
                        t.wait("svec", att_evac)
                    for ic in range(4):
                        mm = t.matmul(PS[:, bank, :], attn_inner[:, ic, ts128],
                                      wout_sb[:, ic, ht * 512:(ht + 1) * 512],
                                      start=(ic == 0), stop=(ic == 3))
                    mm.inc("smm", 1)
                    v.wait("smm", c["smm"])
                    v.tensor_tensor(acc[:, tc, ht * 512:(ht + 1) * 512], PS[:, bank, :],
                                    boutb_sb[:, ht * 512:(ht + 1) * 512], ALU.add)
                    v.dinc("svec", 1)
            attn_state["done_v"] = c["svec"]
            attn_state["done_t"] = c["smm"]

        w2_loads = 0
        p2_last = 0

        def issue_w2_part(e, part):
            nonlocal w2_loads
            if e == 0:
                g.wait("svec", attn_state["done_v"])
            else:
                g.wait("smm", p2_last)
            g.dma_start(out=w2_sb[:, 4 * part:4 * part + 4, :],
                        in_=w2[e][:, 4 * part:4 * part + 4, :]).inc("s2l", 16)
            w2_loads += 1

        unit_done = []
        fslot = ["sfa", "sfb"]
        fin_v = []
        for e in range(NE):
            for j in range(NIC):
                if not (e == 0 and j < 2):
                    issue_w13(e, j)
                if e == 0:
                    if j == 1:
                        emit_comb()
                    if j == 3:
                        emit_cc()
                    if j in (6, 8, 10, 12):
                        issue_w2_part(e, (j - 6) // 2)
                else:
                    if j in (1, 3, 5, 7):
                        issue_w2_part(e, (j - 1) // 2)

                sg, su = (2 * j) % 4, (2 * j + 1) % 4
                t.wait(slot_sems[sg], 16 * slot_uses[sg])
                t.wait(slot_sems[su], 16 * slot_uses[su])
                if e == 0 and j == 0:
                    t.wait("svec", kv_evac_v)
                for tt in range(4):
                    bb = (len(unit_done) % 3) * 2
                    if len(unit_done) >= 3:
                        t.wait("sscl", unit_done[-3][0])
                        t.wait("svec", unit_done[-3][1])
                    for dc in range(NDC):
                        mm = t.matmul(PS[:, bb, :], w13_sb[:, sg, dc, :], xn[:, dc, tt, :],
                                      start=(dc == 0), stop=(dc == NDC - 1))
                    mm.inc("smm", 1)
                    gate_cnt = c["smm"]
                    for dc in range(NDC):
                        mm = t.matmul(PS[:, bb + 1, :], w13_sb[:, su, dc, :], xn[:, dc, tt, :],
                                      start=(dc == 0), stop=(dc == NDC - 1))
                    mm.inc("smm", 1)
                    up_cnt = c["smm"]
                    u = len(unit_done)
                    s.wait("smm", gate_cnt)
                    if u >= 2:
                        s.wait("svec", unit_done[-2][1])
                    s.activation(out=silu_sb[:, u % 2, :], in_=PS[:, bb, :], func=AF.Silu)
                    s.dinc("sscl", 1)
                    v.wait("sscl", c["sscl"])
                    v.wait("smm", up_cnt)
                    v.tensor_mul(act[:, j, tt * 512:(tt + 1) * 512], silu_sb[:, u % 2, :],
                                 PS[:, bb + 1, :])
                    v.dinc("svec", 1)
                    unit_done.append((c["sscl"], c["svec"]))
                slot_freed_at[sg] = up_cnt
                slot_freed_at[su] = up_cnt
                if e == 0 and j == 5:
                    emit_attention()
            act_done = c["svec"]
            act_done_t = c["smm"]

            t.wait("s2l", 16 * w2_loads)
            t.wait("svec", act_done)
            p2_base = c["svec"]
            last = (e == NE - 1)
            fuses = [0, 0]
            for tc in range(NTC):
                ts128 = slice(tc * 128, (tc + 1) * 128)
                if last:
                    sl = tc % 2
                    if tc < 2:
                        sp.wait("smm", act_done_t)
                    else:
                        sp.wait("svec", fin_v[tc - 2])
                    sp.dma_start(out=xfin[:, sl], in_=x_tm[tc]).inc(fslot[sl], 16)
                    fuses[sl] += 1
                for ht in range(2):
                    idx = 2 * tc + ht
                    bank = 6 + idx % 2
                    if idx >= 2:
                        t.wait("svec", p2_base + idx - 1 + (tc - 1 if last else 0))
                    for ic in range(NIC):
                        mm = t.matmul(PS[:, bank, :], act[:, ic, ts128],
                                      w2_sb[:, ic, ht * 512:(ht + 1) * 512],
                                      start=(ic == 0), stop=(ic == NIC - 1))
                    mm.inc("smm", 1)
                    v.wait("smm", c["smm"])
                    v.scalar_tensor_tensor(acc[:, tc, ht * 512:(ht + 1) * 512],
                                           PS[:, bank, :], comb[:, tc, e:e + 1],
                                           acc[:, tc, ht * 512:(ht + 1) * 512],
                                           ALU.mult, ALU.add)
                    v.dinc("svec", 1)
                if last:
                    v.wait(fslot[sl], 16 * fuses[sl])
                    v.tensor_add(acc[:, tc, :], acc[:, tc, :], xfin[:, sl, :])
                    v.dinc("svec", 1)
                    fin_v.append(c["svec"])
                    sp.wait("svec", c["svec"])
                    sp.dma_start(out=y_out[tc], in_=acc[:, tc, :]).inc("sy", 16)
            p2_last = c["smm"]

        ffn_done_v = c["svec"]

        if debug:
            sp.wait("svec", ffn_done_v)
            for tc in range(NTC):
                sp.dma_start(out=dbg["comb"][tc], in_=comb[:, tc, :]).inc("sy", 16)
            for fc in range(4):
                sp.dma_start(out=dbg["qln"][fc], in_=qlnT[:, fc, :]).inc("sy", 16)

        sp.wait("sy", c["sy"])

    with contextlib.ExitStack() as es:
        sems = {nm: es.enter_context(nc.semaphore(nm)) for nm in SEM_NAMES}
        block = es.enter_context(nc.Block())

        @block.tensor
        def _(h):
            emit(Prog(sems, "t", h))

        @block.vector
        def _(h):
            emit(Prog(sems, "v", h))

        @block.scalar
        def _(h):
            emit(Prog(sems, "s", h))

        @block.gpsimd
        def _(h):
            emit(Prog(sems, "g", h))

        @block.sync
        def _(h):
            emit(Prog(sems, "sp", h))

    return nc


def _prep_core_inputs(inputs):
    x = np.asarray(inputs["L0_attn_x"], np.float32).reshape(T, D)
    W_qkv = np.asarray(inputs["L0_attn_W_qkv"], np.float32)
    gq = np.asarray(inputs["L0_attn_gamma_q"], np.float32).reshape(INNER)
    bq = np.asarray(inputs["L0_attn_beta_q"], np.float32).reshape(INNER)
    gk = np.asarray(inputs["L0_attn_gamma_k"], np.float32).reshape(INNER)
    bk = np.asarray(inputs["L0_attn_beta_k"], np.float32).reshape(INNER)
    W_out = np.asarray(inputs["L0_attn_W_out"], np.float32)
    b_out = np.asarray(inputs["L0_attn_b_out"], np.float32)
    gate_w = np.asarray(inputs["L0_ffn_gate_w"], np.float32)
    w13 = np.asarray(inputs["L0_ffn_w13_weight"], np.float32)
    w2 = np.asarray(inputs["L0_ffn_w2_weight"], np.float32)

    wqkv_h = np.ascontiguousarray(W_qkv.T.reshape(NDC, 128, 3 * INNER).transpose(1, 0, 2))
    wout_h = np.ascontiguousarray(W_out.T.reshape(4, 128, D).transpose(1, 0, 2))
    gatew_h = np.ascontiguousarray(gate_w.T.reshape(NDC, 128, NE).transpose(1, 0, 2))
    w13_h = np.ascontiguousarray(
        w13.transpose(0, 2, 1).reshape(NE, NDC, 128, 2 * NIC, 128).transpose(0, 3, 2, 1, 4))
    w2_h = np.ascontiguousarray(
        w2.transpose(0, 2, 1).reshape(NE, NIC, 128, D).transpose(0, 2, 1, 3))
    gqb_h = np.ascontiguousarray(np.broadcast_to(gq, (128, INNER)))
    bqb_h = np.ascontiguousarray(np.broadcast_to(bq, (128, INNER)))
    gkb_h = np.ascontiguousarray(np.broadcast_to(gk, (128, INNER)))
    bkb_h = np.ascontiguousarray(np.broadcast_to(bk, (128, INNER)))
    boutb_h = np.ascontiguousarray(np.broadcast_to(b_out, (128, D)))
    ident_h = np.eye(128, dtype=BF)
    ones_col_h = np.ones((128, 1), BF)
    ones_row_h = np.ones((1, 128), BF)

    in_maps = []
    for cid in range(NCORES):
        sl = x[cid * TPC:(cid + 1) * TPC]
        in_maps.append({
            "x_fm": np.ascontiguousarray(sl.T).reshape(NDC, 128, TPC),
            "x_tm": np.ascontiguousarray(sl).reshape(NTC, 128, D),
            "wqkv": wqkv_h, "wout": wout_h, "gatew": gatew_h,
            "w13": w13_h, "w2": w2_h,
            "gqb": gqb_h, "bqb": bqb_h, "gkb": gkb_h, "bkb": bkb_h,
            "boutb": boutb_h, "ident": ident_h,
            "ones_col": ones_col_h, "ones_row": ones_row_h,
        })
    return in_maps


_NC_CACHE = {}


def kernel(**inputs):
    if "nc" not in _NC_CACHE:
        _NC_CACHE["nc"] = build_nc(debug=False)
    nc = _NC_CACHE["nc"]
    in_maps = _prep_core_inputs(inputs)
    res = run_bass_kernel_spmd(nc, in_maps, core_ids=list(range(NCORES)))
    out = np.empty((T, D), np.float32)
    for cid in range(NCORES):
        out[cid * TPC:(cid + 1) * TPC] = res.results[cid]["y"].reshape(TPC, D)
    return out.reshape(B, N, D)
